# version 10
# baseline (speedup 1.0000x reference)
import contextlib
import numpy as np
import ml_dtypes
import concourse.bass as bass
import concourse.mybir as mybir
from concourse.bass_utils import run_bass_kernel_spmd

F32 = mybir.dt.float32
BF16 = mybir.dt.bfloat16
AF = mybir.ActivationFunctionType
ALU = mybir.AluOpType
AX = mybir.AxisListType

T = 2048
D = 1024
DC = D // 128
NTT = T // 128
E = 8
FF = 2048
FH = FF // 2
NSUB = 2 * E
FS = FH // 128
NG = T // 512
NH = 16
NP = NH // 2
HD = 64
LH = 4
DIN = 256
NCK = 2
CHK = 1024
CCT = 8
EPS = 1e-5
BIG = 1e30

PHASES = (1, 2, 3)
DEBUG = False
ENGINES = ("sync", "gpsimd", "tensor", "vector", "scalar")


class Cnt:
    __slots__ = ("sem", "v")
    def __init__(self, sem):
        self.sem = sem
        self.v = 0


class Emit:
    def __init__(self, nc, sems):
        self.nc = nc
        self.sems = sems
        self.cur = None
        self.waited = {}

    def run(self, name, gen):
        snap = {n: c.v for n, c in self.sems.items()}
        with self.nc.Block(name) as blk:
            for en in ENGINES:
                for n, c in self.sems.items():
                    c.v = snap[n]
                self.cur = en
                self.waited = {}
                getattr(blk, en)(lambda _e, g=gen: g())
        self.cur = None

    def _wait(self, waits):
        eng = getattr(self.nc, self.cur)
        for c, v in waits:
            if v is None or v <= 0:
                continue
            k = c.sem.num
            if self.waited.get(k, -1) >= v:
                continue
            eng.wait_ge(c.sem, v)
            self.waited[k] = v

    def _fin(self, i, inc):
        if inc is None:
            return None
        if isinstance(inc, tuple):
            c, n = inc
        else:
            c, n = inc, 1
        if i is not None:
            i.then_inc(c.sem, n)
        c.v += n
        return c.v

    def _go(self, en, waits):
        if self.cur != en:
            return False
        self._wait(waits)
        return True

    def dma(self, q, out, in_, sem, waits=(), accum=False, serial=False):
        i = None
        if serial:
            waits = list(waits) + [(sem, sem.v)]
        if self._go(q, waits):
            kw = {}
            if accum:
                kw["accum_op"] = ALU.add
            i = getattr(self.nc, q).dma_start(out=out, in_=in_, **kw)
        return self._fin(i, (sem, 16))

    def mm(self, out, lhsT, rhs, start, stop, inc=None, waits=(), **kw):
        i = None
        if self._go("tensor", waits):
            i = self.nc.tensor.matmul(out, lhsT, rhs, start=start, stop=stop,
                                      **kw)
        return self._fin(i, inc)

    def act(self, out, in_, func, bias=None, scale=None, accum_out=None,
            inc=None, waits=()):
        i = None
        if self._go("scalar", waits):
            kw = {}
            if bias is not None:
                kw["bias"] = bias
            if scale is not None:
                kw["scale"] = scale
            if accum_out is not None:
                kw["accum_out"] = accum_out
            i = self.nc.scalar.activation(out, in_, func, **kw)
        return self._fin(i, inc)

    def vop(self, fn, *args, inc=None, waits=(), **kw):
        i = None
        if self._go("vector", waits):
            i = getattr(self.nc.vector, fn)(*args, **kw)
        return self._fin(i, inc)

    def wait_on(self, en, waits):
        if self.cur == en:
            self._wait(waits)


class Builder:
    def __init__(self):
        nc = bass.Bass("TRN2", target_bir_lowering=False)
        self.nc = nc
        dt_ = nc.dram_tensor
        self.x = dt_("x", [T, D], F32, kind="ExternalInput").ap()
        self.Wg = dt_("ffn_wg", [E, D], F32, kind="ExternalInput").ap()
        self.W1 = dt_("ffn_w1", [E, FF, D], F32, kind="ExternalInput").ap()
        self.W2 = dt_("ffn_w2", [E, FF, D], F32, kind="ExternalInput").ap()
        self.W3 = dt_("ffn_w3", [E, D, FF], F32, kind="ExternalInput").ap()
        self.ln_w = dt_("ln_w", [D], F32, kind="ExternalInput").ap()
        self.ln_b = dt_("ln_b", [D], F32, kind="ExternalInput").ap()
        self.ipw = dt_("in_proj_w", [3 * D, D], F32, kind="ExternalInput").ap()
        self.ipb = dt_("in_proj_b", [3 * D], F32, kind="ExternalInput").ap()
        self.opw = dt_("out_proj_w", [D, D], F32, kind="ExternalInput").ap()
        self.opb = dt_("out_proj_b", [D], F32, kind="ExternalInput").ap()
        self.tw0 = dt_("ttt_w0", [LH, DIN, DIN], F32, kind="ExternalInput").ap()
        self.tw1 = dt_("ttt_w1", [LH, DIN, DIN], F32, kind="ExternalInput").ap()
        self.tw2 = dt_("ttt_w2", [LH, DIN, DIN], F32, kind="ExternalInput").ap()
        self.identD = dt_("identD", [128, 128], BF16, kind="ExternalInput").ap()
        self.selbD = dt_("selbD", [2, 256], BF16, kind="ExternalInput").ap()
        self.selfD = dt_("selfD", [2, 128], F32, kind="ExternalInput").ap()
        self.y = dt_("y", [T, D], F32, kind="ExternalOutput").ap()
        kk = "ExternalOutput" if DEBUG else "Internal"
        self.acc = dt_("acc_scr", [T, D], F32, kind=kk).ap()
        self.x2s = dt_("x2_scr", [T, D], F32, kind=kk).ap()
        self.lacts = dt_("lact_scr", [T, D], F32, kind=kk).ap()
        self.dbg = dt_("dbg", [128, 28 * 1024], F32,
                       kind="ExternalOutput" if DEBUG else "Internal").ap()
        self.w1s = dt_("w1t_scr", [E, 2, 128, FS * D], BF16, kind="Internal").ap()
        self.w2s = dt_("w2t_scr", [E, 2, 128, FS * D], BF16, kind="Internal").ap()
        self.w3s = dt_("w3t_scr", [E, 2, 128, FS * D], BF16, kind="Internal").ap()

        self.es = contextlib.ExitStack()
        self.sems = {}

    def sem(self, name):
        if name not in self.sems:
            self.sems[name] = Cnt(self.es.enter_context(self.nc.semaphore(name)))
        return self.sems[name]

    def sb(self, name, shape, dtype, es=None):
        return (es or self.es).enter_context(self.nc.sbuf_tensor(name, shape, dtype))

    def psum(self, name, shape, es=None):
        return (es or self.es).enter_context(self.nc.psum_tensor(name, shape, F32))

    def build(self):
        nc = self.nc
        with self.es:
            self.ident = self.sb("ident", [128, 128], BF16)
            self.ones_f = self.sb("ones_f", [1, 128], F32)
            names = (["xld0", "xld1", "nat", "nat1", "nat2", "nat3", "nat4",
                      "nat5", "wld0", "wld1", "scw0", "scw1", "init",
                      "tm", "ve", "se", "wgld", "pj0", "pj1", "xe0", "xe1",
                      "lctw", "ttw"]
                     + [f"rmw{i}" for i in range(NTT)])
            for n in names:
                self.sem(n)
            lo = min(c.sem.num for c in self.sems.values())
            hi = max(c.sem.num for c in self.sems.values())
            nc.gpsimd.dma_reset(range(lo, hi + 1))
            nc.gpsimd.sem_clear(range(lo, hi + 1))
            nc.all_engine_barrier()

            E_ = Emit(nc, self.sems)

            def prolog():
                t = E_.dma("gpsimd", self.ident[:], self.identD,
                           self.sem("wgld"))
                E_.vop("memset", self.ones_f[:], 1.0, inc=self.sem("ve"))
                E_.wait_on("gpsimd", [(self.sem("wgld"), t)])
            E_.run("prolog", prolog)

            if 1 in PHASES:
                self.ffn_phase(E_, 1, src=self.x, dst=self.acc, build_scratch=True)
            if 2 in PHASES:
                self.attn_phase(E_, src=self.acc if 1 in PHASES else self.x,
                                dst=self.x2s)
            if 3 in PHASES:
                self.ffn_phase(E_, 3, src=self.x2s if 2 in PHASES else self.x,
                               dst=self.y,
                               build_scratch=1 not in PHASES)
            else:
                src = self.x2s if 2 in PHASES else self.acc

                def copyout():
                    fin = self.sem("init")
                    t = E_.dma("sync", self.y, src, fin)
                    E_.wait_on("sync", [(fin, t)])
                E_.run("copyout", copyout)
        return nc

    def ffn_phase(self, E_, pidx, src, dst, build_scratch):
        nc = self.nc
        tag = f"f{pidx}_"
        st = contextlib.ExitStack()
        with st:
            sb = lambda n, sh, dt: self.sb(tag + n, sh, dt, st)
            psl = lambda n, sh: self.psum(tag + n, sh, st)

            xnT = sb("xnT", [128, DC * T], BF16)
            w1b = [sb(f"w1b{i}", [128, FS * D], BF16) for i in range(2)]
            w2b = [sb(f"w2b{i}", [128, FS * D], BF16) for i in range(2)]
            w3b = [sb(f"w3b{i}", [128, FS * D], BF16) for i in range(2)]
            hid = [sb(f"hid{i}", [128, FS * 512], BF16) for i in range(2)]
            asb = [sb("asb0", [128, 512], BF16)]
            tmp = [sb(f"tmp{i}", [128, D], F32) for i in range(2)]
            xst = [sb(f"xst{i}", [128, D], F32) for i in range(2)]
            xns = sb("xns", [128, D], BF16)
            nat = ([sb(f"nat{chr(65 + i)}", [128, 2048], BF16)
                    for i in range(6)] if build_scratch else None)
            wgb = sb("wgb", [E, D], BF16)
            wgT = sb("wgT", [128, DC * E], BF16)
            gates = sb("gates", [128, E * NTT], F32)
            stats = sb("stats", [128, 12], F32)
            mv = sb("mv", [128, 2], F32)
            veps = sb("veps", [128, 1], F32)
            lnv = sb("lnv", [128, 1], F32)
            rstd = sb("rstd", [128, NTT], F32)
            sc_all = sb("sc", [128, NTT * E], F32)
            m1a = sb("m1a", [128, NTT * E], F32)
            g8 = sb("g8", [128, E], F32)
            g8b = sb("g8b", [128, E], F32)
            gsc = sb("gsc", [128, 128], F32)

            ps_h1 = psl("h1", [128, 512])
            ps_h2 = psl("h2", [128, 512])
            ps_oa = psl("oa", [128, D])
            ps_ob = psl("ob", [128, D])
            ps_tr = psl("tr", [128, 512])
            ps_aux = psl("aux", [128, 512])
            ps_g = ps_aux

            s_x = [self.sem("xld0"), self.sem("xld1")]
            s_wld = [self.sem("wld0"), self.sem("wld1")]
            s_nat = [self.sem("nat"), self.sem("nat1"), self.sem("nat2"),
                     self.sem("nat3"), self.sem("nat4"), self.sem("nat5")]
            s_scw = [self.sem("scw0"), self.sem("scw1")]
            s_init = self.sem("init")
            s_wg = self.sem("wgld")
            s_tm = self.sem("tm")
            s_ve = self.sem("ve")
            s_se = self.sem("se")
            s_rmw = [self.sem(f"rmw{i}") for i in range(NTT)]

            def gen():
                t_init = E_.dma("sync", dst, src, s_init)
                t_wgl = E_.dma("gpsimd", wgb[:], self.Wg, s_wg)
                for dc in range(DC):
                    E_.mm(ps_tr[:, dc * E:(dc + 1) * E],
                          wgb[0:E, dc * 128:(dc + 1) * 128],
                          self.ident[0:E, 0:E], start=True, stop=True,
                          inc=s_tm if dc == DC - 1 else None,
                          waits=[(s_wg, t_wgl)])
                t_wgt = s_tm.v
                t_wge = E_.vop("tensor_copy", wgT[:], ps_tr[:, 0:DC * E],
                               inc=s_ve, waits=[(s_tm, t_wgt)])

                xld = [None] * NTT
                apl = [None] * NTT
                lnt = [None] * NTT
                scdone = [None] * NTT
                prev_tr_evac = t_wge
                prev_g_evac = None
                for tt in range(NTT):
                    p = tt % 2
                    xld[tt] = E_.dma(
                        "sync", xst[p][:], src[tt * 128:(tt + 1) * 128, :],
                        s_x[p], serial=True,
                        waits=[(s_ve, apl[tt - 2])] if tt >= 2 else [])
                    E_.vop("bn_stats", stats[:, 0:6], xst[p][:, 0:512],
                           waits=[(s_x[p], xld[tt])])
                    tbn = E_.vop("bn_stats", stats[:, 6:12],
                                 xst[p][:, 512:1024], inc=s_ve)
                    tag_ = E_.vop("bn_aggr", mv[:], stats[:, 0:12], inc=s_ve,
                                  waits=[(s_ve, tbn)])
                    tve = E_.vop("tensor_single_scalar", veps[:], mv[:, 1:2],
                                 EPS, ALU.add, inc=s_ve,
                                 waits=[(s_ve, tag_)])
                    tln = E_.act(lnv[:], veps[:], AF.Ln, inc=s_se,
                                 waits=[(s_ve, tve)])
                    tsq = E_.act(rstd[:, tt:tt + 1], lnv[:], AF.Exp,
                                 scale=-0.5, inc=s_se, waits=[(s_se, tln)])
                    apl[tt] = E_.vop(
                        "tensor_scalar", xns[:], xst[p][:],
                        mv[:, 0:1], rstd[:, tt:tt + 1],
                        op0=ALU.subtract, op1=ALU.mult, inc=s_ve,
                        waits=[(s_se, tsq), (s_ve, tag_)] +
                              ([(s_tm, lnt_mm)] if tt >= 1 else []))
                    for dcg in range(2):
                        for dcl in range(4):
                            dc = dcg * 4 + dcl
                            E_.mm(ps_tr[:, dcl * 128:(dcl + 1) * 128],
                                  xns[:, dc * 128:(dc + 1) * 128],
                                  self.ident[:], start=True, stop=True,
                                  inc=s_tm if dcl == 3 else None,
                                  waits=[(s_ve, apl[tt]),
                                         (s_ve, prev_tr_evac)])
                        tmm = s_tm.v
                        xv = xnT.ap().rearrange("p (dc t) -> p dc t", dc=DC)
                        prev_tr_evac = E_.vop(
                            "tensor_copy",
                            xv[:, dcg * 4:(dcg + 1) * 4,
                               tt * 128:(tt + 1) * 128],
                            ps_tr.ap().rearrange("p (a b) -> p a b", a=4),
                            inc=s_ve, waits=[(s_tm, tmm)])
                    lnt_mm = s_tm.v
                    lnt[tt] = prev_tr_evac
                    for dc in range(DC):
                        E_.mm(ps_g[:, 0:E],
                              xnT[:, dc * T + tt * 128:
                                  dc * T + (tt + 1) * 128],
                              wgT[:, dc * E:(dc + 1) * E],
                              start=(dc == 0), stop=(dc == DC - 1),
                              inc=s_tm if dc == DC - 1 else None,
                              waits=[(s_ve, t_wge), (s_ve, prev_g_evac),
                                     (s_ve, lnt[tt])])
                    tgs = s_tm.v
                    prev_g_evac = E_.vop(
                        "tensor_copy", sc_all[:, tt * E:(tt + 1) * E],
                        ps_g[:, 0:E], inc=s_ve, waits=[(s_tm, tgs)])
                    t1 = E_.vop("reduce_max", gsc[:, tt:tt + 1],
                                sc_all[:, tt * E:(tt + 1) * E], axis=AX.X,
                                inc=s_ve, waits=[(s_ve, prev_g_evac)])
                    t2 = E_.vop("tensor_scalar", m1a[:, tt * E:(tt + 1) * E],
                                sc_all[:, tt * E:(tt + 1) * E],
                                gsc[:, tt:tt + 1], None, op0=ALU.is_ge,
                                inc=s_ve, waits=[(s_ve, t1)])
                    t3 = E_.vop("scalar_tensor_tensor", g8[:],
                                m1a[:, tt * E:(tt + 1) * E], -BIG,
                                sc_all[:, tt * E:(tt + 1) * E],
                                op0=ALU.mult, op1=ALU.add,
                                inc=s_ve, waits=[(s_ve, t2)])
                    E_.vop("reduce_max", gsc[:, NTT + tt:NTT + 1 + tt], g8[:],
                           axis=AX.X, inc=s_ve, waits=[(s_ve, t3)])

                tdd = E_.vop("tensor_tensor", gsc[:, 2*NTT:3*NTT], gsc[:, 0:NTT],
                             gsc[:, NTT:2*NTT], op=ALU.subtract, inc=s_ve,
                             waits=[(s_ve, s_ve.v)])
                tsg = E_.act(gsc[:, 3*NTT:4*NTT], gsc[:, 2*NTT:3*NTT], AF.Sigmoid,
                             inc=s_se, waits=[(s_ve, tdd)])
                t4 = E_.vop("tensor_single_scalar", gsc[:, 4*NTT:5*NTT],
                            gsc[:, 3*NTT:4*NTT], 0.5, ALU.mult, inc=s_ve,
                            waits=[(s_se, tsg)])
                t5 = E_.vop("tensor_scalar", gsc[:, 5*NTT:6*NTT], gsc[:, 4*NTT:5*NTT],
                            -1.0, 0.5, op0=ALU.mult, op1=ALU.add, inc=s_ve,
                            waits=[(s_ve, t4)])
                t6 = E_.vop("tensor_tensor", gsc[:, 6*NTT:7*NTT], gsc[:, 4*NTT:5*NTT],
                            gsc[:, 5*NTT:6*NTT], op=ALU.subtract, inc=s_ve,
                            waits=[(s_ve, t5)])
                gv = gates.ap().rearrange("p (e t) -> p e t", e=E)
                t_gt = None
                tprev = t6
                for tt in range(NTT):
                    ta = E_.vop("tensor_scalar", g8[:],
                                sc_all[:, tt * E:(tt + 1) * E],
                                gsc[:, NTT + tt:NTT + 1 + tt], None, op0=ALU.is_ge,
                                inc=s_ve, waits=[(s_ve, tprev)])
                    tb = E_.vop("tensor_scalar", g8b[:], g8[:],
                                gsc[:, 5*NTT + tt:5*NTT + 1 + tt], None, op0=ALU.mult,
                                inc=s_ve, waits=[(s_ve, ta)])
                    t_gt = E_.vop("scalar_tensor_tensor",
                                  gv[:, :, tt:tt + 1],
                                  m1a[:, tt * E:(tt + 1) * E],
                                  gsc[:, 6*NTT + tt:6*NTT + 1 + tt], g8b[:],
                                  op0=ALU.mult, op1=ALU.add, inc=s_ve,
                                  waits=[(s_ve, tb)])
                    tprev = t_gt

                wready = [None] * NSUB
                wsem = [None] * NSUB
                sub_done = [None] * NSUB
                scw_tok = [None] * NSUB
                trv_tok = [None] * NSUB
                prev_sil = None
                prev_htt = None
                prev_h1mm = None
                prev_h2mm = None
                hid_user = [None, None]
                oev_last = [None, None]
                rmw_t = [None] * NTT
                prev_tr_mm = lnt_mm

                nat_tr = [None] * 6
                nat_ctr = [0]
                nat_tok = {}

                def weights_dma(k):
                    e, fh = k // 2, k % 2
                    p = k % 2
                    if not build_scratch:
                        wt = None
                        for wsrc, wdst in ((self.w1s, w1b), (self.w2s, w2b),
                                           (self.w3s, w3b)):
                            wt = E_.dma(
                                "sync", wdst[p][:], wsrc[e, fh], s_wld[p],
                                waits=([(s_tm, sub_done[k - 2])]
                                       if k >= 2 else []))
                        wready[k] = wt
                        wsem[k] = s_wld[p]
                        return
                    for mi, wsrcap in enumerate((
                            self.W1[e, fh * FH:(fh + 1) * FH, :]
                            .rearrange("(a p) d -> p a d", p=128),
                            self.W2[e, fh * FH:(fh + 1) * FH, :]
                            .rearrange("(a p) d -> p a d", p=128),
                            self.W3[e, :, fh * FH:(fh + 1) * FH]
                            .rearrange("(a p) f -> p a f", p=128))):
                        n_in = DC if mi == 2 else FS
                        wid = FH if mi == 2 else D
                        pieces = 4 if n_in >= 4 else 2
                        pb = n_in // pieces
                        for pc in range(pieces):
                            nb = nat_ctr[0] % 6
                            nat_ctr[0] += 1
                            nat_tok[(k, mi, pc)] = (nb, E_.dma(
                                "gpsimd",
                                nat[nb].ap()[:, 0:pb * wid].rearrange(
                                    "p (a d) -> p a d", a=pb),
                                wsrcap[:, pc * pb:(pc + 1) * pb, :],
                                s_nat[nb], waits=[(s_tm, nat_tr[nb])]))

                def weights_gen(k):
                    nonlocal prev_tr_evac, prev_tr_mm
                    e, fh = k // 2, k % 2
                    p = k % 2
                    if not build_scratch:
                        return
                    plist = []
                    wdsts = (w1b, w2b, w3b)
                    wscrs = (self.w1s, self.w2s, self.w3s)
                    srcs = (self.W1[e, fh * FH:(fh + 1) * FH, :]
                            .rearrange("(a p) d -> p a d", p=128),
                            self.W2[e, fh * FH:(fh + 1) * FH, :]
                            .rearrange("(a p) d -> p a d", p=128),
                            self.W3[e, :, fh * FH:(fh + 1) * FH]
                            .rearrange("(a p) f -> p a f", p=128))
                    for mi in range(3):
                        n_in = DC if mi == 2 else FS
                        pieces = 4 if n_in >= 4 else 2
                        for pc in range(pieces):
                            plist.append((mi, pc))
                    npc = len(plist)

                    def emit_dma(qi):
                        mi, pc = plist[qi]
                        n_in = DC if mi == 2 else FS
                        wid = FH if mi == 2 else D
                        pieces = 4 if n_in >= 4 else 2
                        pb = n_in // pieces
                        nb = nat_ctr[0] % 6
                        nat_ctr[0] += 1
                        nat_tok[(k, mi, pc)] = (nb, E_.dma(
                            "gpsimd",
                            nat[nb].ap()[:, 0:pb * wid].rearrange(
                                "p (a d) -> p a d", a=pb),
                            srcs[mi][:, pc * pb:(pc + 1) * pb, :],
                            s_nat[nb], waits=[(s_tm, nat_tr[nb])]))

                    for qi in range(min(6, npc)):
                        emit_dma(qi)
                    yield
                    ogrp = [0]
                    aux0 = prev_tr_evac
                    if prev_g_evac is not None:
                        aux0 = max(aux0 or 0, prev_g_evac)
                    tr_ev2 = [prev_tr_evac, aux0]
                    for qi, (mi, pc) in enumerate(plist):
                        is_w3 = (mi == 2)
                        n_out = FS if is_w3 else DC
                        n_in = DC if is_w3 else FS
                        pieces = 4 if n_in >= 4 else 2
                        pb = n_in // pieces
                        nb, tn = nat_tok[(k, mi, pc)]
                        natb = nat[nb]
                        wts = [(s_nat[nb], tn)]
                        if k >= 2:
                            wts += [(s_tm, sub_done[k - 2]),
                                    (s_scw[p], scw_tok[k - 2])]
                        for o in range(n_out):
                            bank = (ps_tr, ps_aux)[ogrp[0] % 2]
                            for bl in range(pb):
                                scol = bl * (FH if is_w3 else D) + o * 128
                                E_.mm(bank[:, bl * 128:(bl + 1) * 128],
                                      natb[:, scol:scol + 128],
                                      self.ident[:], start=True, stop=True,
                                      inc=(s_tm if bl == pb - 1 else None),
                                      waits=wts +
                                            [(s_ve, tr_ev2[ogrp[0] % 2])])
                            prev_tr_mm = s_tm.v
                            nat_tr[nb] = s_tm.v
                            if is_w3:
                                dcol = o * D + pc * pb * 128
                            else:
                                dcol = o * FH + pc * pb * 128
                            tr_ev2[ogrp[0] % 2] = E_.vop(
                                "tensor_copy",
                                wdsts[mi][p][:, dcol:dcol + pb * 128],
                                bank[:, 0:pb * 128],
                                inc=s_ve, waits=[(s_tm, prev_tr_mm)])
                            prev_tr_evac = tr_ev2[ogrp[0] % 2]
                            ogrp[0] += 1
                        if qi + 6 < npc:
                            emit_dma(qi + 6)
                        if pc == pieces - 1:
                            scw_tok[k] = E_.dma(
                                "sync", wscrs[mi][e, fh], wdsts[mi][p][:],
                                s_scw[p], waits=[(s_ve, prev_tr_evac)])
                        yield
                    trv_tok[k] = prev_tr_evac
                    wready[k] = prev_tr_evac
                    wsem[k] = s_ve

                def compute(k, feed=None):
                    nonlocal prev_sil, prev_htt, prev_h1mm, prev_h2mm
                    e, p = k // 2, k % 2
                    if feed is not None:
                        next(feed, None)
                    for g in range(NG):
                        hb = hid[g % 2]
                        for fs in range(FS):
                            w_h = [(wsem[k], wready[k])]
                            for dc in range(DC):
                                E_.mm(ps_h1[:],
                                      w1b[p][:, dc * FH + fs * 128:
                                             dc * FH + (fs + 1) * 128],
                                      xnT[:, dc * T + g * 512:
                                          dc * T + (g + 1) * 512],
                                      start=(dc == 0), stop=(dc == DC - 1),
                                      inc=s_tm if dc == DC - 1 else None,
                                      waits=w_h + [(s_se, prev_sil),
                                                   (s_ve, lnt[g * 4 + 3])])
                            t_h1 = s_tm.v
                            for dc in range(DC):
                                E_.mm(ps_h2[:],
                                      w2b[p][:, dc * FH + fs * 128:
                                             dc * FH + (fs + 1) * 128],
                                      xnT[:, dc * T + g * 512:
                                          dc * T + (g + 1) * 512],
                                      start=(dc == 0), stop=(dc == DC - 1),
                                      inc=s_tm if dc == DC - 1 else None,
                                      waits=[(s_ve, prev_htt)])
                            t_h2 = s_tm.v
                            prev_sil = E_.act(
                                asb[0][:], ps_h1[:], AF.Silu, inc=s_se,
                                waits=[(s_tm, t_h1),
                                       (s_ve, prev_htt)])
                            prev_htt = E_.vop(
                                "tensor_tensor",
                                hb[:, fs * 512:(fs + 1) * 512],
                                asb[0][:], ps_h2[:], op=ALU.mult,
                                inc=s_ve,
                                waits=[(s_se, prev_sil), (s_tm, t_h2),
                                       (s_tm, hid_user[g % 2])])
                        htt_all = prev_htt
                        for tc in range(4):
                            tile = g * 4 + tc
                            pso = ps_oa if tc % 2 == 0 else ps_ob
                            wts = [(s_ve, htt_all),
                                   (s_se, oev_last[tc % 2])]
                            for fs in range(FS):
                                for dh in range(2):
                                    E_.mm(pso[:, dh * 512:(dh + 1) * 512],
                                          hb[:, fs * 512 + tc * 128:
                                             fs * 512 + (tc + 1) * 128],
                                          w3b[p][:, fs * D + dh * 512:
                                                 fs * D + (dh + 1) * 512],
                                          start=(fs == 0), stop=(fs == FS - 1),
                                          inc=(s_tm if fs == FS - 1 and
                                               dh == 1 else None),
                                          waits=wts)
                            t_out = s_tm.v
                            hid_user[g % 2] = t_out
                            slot = tile % 2
                            ev = E_.act(
                                tmp[slot][:], pso[:], AF.Copy, bias=0.0,
                                scale=gates[:, e * NTT + tile:
                                            e * NTT + tile + 1],
                                inc=s_se,
                                waits=[(s_tm, t_out), (s_ve, t_gt),
                                       (s_rmw[(tile - 2) % NTT],
                                        rmw_t[(tile - 2) % NTT])])
                            oev_last[tc % 2] = ev
                            rmw_t[tile] = E_.dma(
                                "gpsimd", dst[tile * 128:(tile + 1) * 128, :],
                                tmp[slot][:], s_rmw[tile], accum=True,
                                waits=[(s_se, ev), (s_init, t_init),
                                       (s_rmw[tile], rmw_t[tile])])
                        if feed is not None:
                            for _ in range(4):
                                next(feed, None)
                    sub_done[k] = s_tm.v

                if build_scratch:
                    for _ in weights_gen(0):
                        pass
                    for k in range(NSUB):
                        feed = weights_gen(k + 1) if k + 1 < NSUB else None
                        compute(k, feed=feed)
                        if feed is not None:
                            for _ in feed:
                                pass
                else:
                    weights_dma(0)
                    weights_dma(1)
                    for k in range(NSUB):
                        compute(k)
                        if k + 2 < NSUB:
                            weights_dma(k + 2)

                for i in range(NTT):
                    E_.wait_on("gpsimd", [(s_rmw[i], rmw_t[i])])
                E_.wait_on("gpsimd",
                           [(s, s.v) for s in s_nat] + [(s_wg, s_wg.v)])
                E_.wait_on("sync", [(s_scw[0], s_scw[0].v),
                                    (s_scw[1], s_scw[1].v),
                                    (s_wld[0], s_wld[0].v),
                                    (s_wld[1], s_wld[1].v),
                                    (s_init, t_init)])
                for p in range(2):
                    E_.wait_on("sync", [(s_x[p], s_x[p].v)])

            E_.run(tag + "blk", gen)

    def attn_phase(self, E_, src, dst):
        nc = self.nc
        st1 = contextlib.ExitStack()
        with st1:
            sb = lambda n, sh, dt: self.sb("a_" + n, sh, dt, st1)
            xnT = sb("xnT", [128, DC * T], BF16)
            y_st = sb("yst", [128, D], F32)
            xst = [sb(f"xst{i}", [128, D], F32) for i in range(2)]
            stats = sb("stats", [128, 12], F32)
            mv = sb("mv", [128, 2], F32)
            veps = sb("veps", [128, 1], F32)
            lnv = sb("lnv", [128, 1], F32)
            rstd = sb("rstd", [128, 2], F32)
            gbc = sb("gbc", [128, D], F32)
            bbc = sb("bbc", [128, D], F32)
            obc = sb("obc", [128, D], F32)
            inb = sb("inb", [128, 24], F32)
            vbc = sb("vbc", [128, D], F32)
            vbc4 = sb("vbc4", [128, 512], F32)
            ls = [sb(f"ls{i}", [128, DIN], F32) for i in range(2)]
            negb = sb("negb", [128, NH], F32)
            negbp = sb("negbp", [128, NP], F32)
            rr_s = sb("rr_s", [2, 512], F32)
            nrm = sb("nrm", [2, 16], F32)
            bnd = sb("bnd", [2, 8], F32)
            bndb = sb("bndb", [2, 1], BF16)
            ones2 = sb("ones2", [128, 2], BF16)
            selb = sb("selb", [2, 256], BF16)
            onz = sb("onz", [128, 2], BF16)
            zno = sb("zno", [128, 2], BF16)
            onescol = sb("onescol", [128, 1], BF16)
            self_f = sb("self", [2, 128], F32)
            rr2 = sb("rr2", [2, 512], F32)
            rbc = sb("rbc", [128, 512], F32)

            s_x = [self.sem("xld0"), self.sem("xld1")]
            s_pj = [self.sem("pj0"), self.sem("pj1")]
            s_xe = [self.sem("xe0"), self.sem("xe1")]
            s_lc = [self.sem("lc0"), self.sem("lc1")]
            s_lr = [self.sem("lr0"), self.sem("lr1")]
            s_tw = self.sem("ttw")
            s_tm = self.sem("tm")
            s_ve = self.sem("ve")
            s_se = self.sem("se")

            stP = contextlib.ExitStack()
            inbf = self.sb("p_inbf", [1, 3 * D], F32, stP)
            lnwf = self.sb("p_lnwf", [1, D], F32, stP)
            lnbf = self.sb("p_lnbf", [1, D], F32, stP)
            opbf = self.sb("p_opbf", [1, D], F32, stP)
            stP.close()
            st2 = contextlib.ExitStack()
            sb2 = lambda n, sh, dt: self.sb("t_" + n, sh, dt, st2)
            xtok = sb2("xtok", [128, NTT * D], BF16)
            w0T = sb2("w0T", [128, LH * 2 * DIN], BF16)
            w1a = sb2("w1a", [128, LH * 2 * DIN], BF16)
            w2T = sb2("w2T", [128, LH * 2 * DIN], BF16)
            twn = sb2("twn", [128, 2 * DIN], BF16)
            w0nT = sb2("w0nT", [128, 2 * DIN], BF16)
            w2nT = sb2("w2nT", [128, 2 * DIN], BF16)
            w1n = sb2("w1n", [128, 2 * DIN], BF16)
            w1nT = sb2("w1nT", [128, 2 * DIN], BF16)
            gt_s = sb2("gt", [128, 2048], F32)
            sg_s = sb2("sg", [128, 2048], F32)
            g_s = sb2("g", [128, 2048], F32)
            dh_s = sb2("dh", [128, 2048], F32)
            h_s = sb2("h", [128, 2048], F32)
            hq_s = sb2("hq", [128, 2048], F32)
            hid_b = sb2("hid", [128, 2048], BF16)
            dpre_b = sb2("dpre", [128, 2048], BF16)
            dg_b = sb2("dg", [128, 2048], BF16)
            hg_b = sb2("hg", [128, 2048], BF16)
            hgT_b = sb2("hgT", [128, 2048], BF16)
            pst2 = contextlib.ExitStack()
            ps_tr = self.psum("a_tr", [128, 512], pst2)
            pA = self.psum("a_pA", [128, 512], pst2)
            pB = self.psum("a_pB", [128, 512], pst2)
            pC = self.psum("a_pC", [128, 512], pst2)
            st2.close()
            pst2.close()
            st3 = contextlib.ExitStack()
            sb3 = lambda n, sh, dt: self.sb("q_" + n, sh, dt, st3)
            Qp = [sb3(f"Qp{i}", [128, T], BF16) for i in range(2)]
            Kp = [sb3(f"Kp{i}", [128, T], BF16) for i in range(2)]
            Vp = [sb3(f"Vp{i}", [128, NTT * 128], BF16) for i in range(2)]
            pjn = sb3("pjn", [128, 3 * D], BF16)
            pjT = [sb3(f"pjT{i}", [128, DC * 384], BF16) for i in range(2)]
            PT = [sb3(f"PT{i}", [128, 1024], BF16) for i in range(2)]
            sqb = sb3("sqb", [128, T], BF16)
            aoT = sb3("aoT", [128, DC * T], BF16)
            opT = sb3("opT", [128, DC * D], BF16)
            x1t = [sb3(f"x1t{i}", [128, D], F32) for i in range(2)]
            lct = [sb3(f"lct{i}", [128, D], F32) for i in range(2)]
            x2t = [sb3(f"x2t{i}", [128, D], F32) for i in range(2)]
            pst3 = contextlib.ExitStack()
            stAB = [self.psum(f"a_st{i}", [128, 1024], pst3) for i in range(2)]
            pv = [self.psum(f"a_pv{i}", [128, 512], pst3) for i in range(2)]
            dn = self.psum("a_dn", [128, 512], pst3)
            qkb = self.psum("a_qkb", [128, 512], pst3)
            pst3.close()

            def gen():
                E_.vop("memset", ones2[0:64, 0:1], 1.0)
                E_.vop("memset", ones2[64:128, 0:1], 0.0)
                E_.vop("memset", ones2[0:64, 1:2], 0.0)
                E_.vop("memset", ones2[64:128, 1:2], 1.0)
                E_.vop("memset", onescol[:], 1.0)
                E_.vop("memset", onz[:, 0:1], 1.0)
                E_.vop("memset", onz[:, 1:2], 0.0)
                E_.vop("memset", zno[:, 0:1], 0.0)
                t_ms = E_.vop("memset", zno[:, 1:2], 1.0, inc=s_ve)
                E_.dma("gpsimd", selb[:], self.selbD, s_tw)
                E_.dma("gpsimd", self_f[:], self.selfD, s_tw)

                E_.dma("gpsimd", lnwf[:], self.ln_w.rearrange("(a d) -> a d", a=1), s_tw)
                E_.dma("gpsimd", lnbf[:], self.ln_b.rearrange("(a d) -> a d", a=1), s_tw)
                E_.dma("gpsimd", opbf[:], self.opb.rearrange("(a d) -> a d", a=1), s_tw)
                E_.dma("gpsimd", inbf[:], self.ipb.rearrange("(a d) -> a d", a=1), s_tw)

                t_small = s_tw.v

                prev_ev = t_ms
                for (flat, off, dstb) in ((lnwf, 0, gbc), (lnbf, 0, bbc),
                                          (opbf, 0, obc),
                                          (inbf, 2 * D, vbc)):
                    for cg in range(2):
                        E_.mm(ps_tr[:], self.ones_f[0:1, :],
                              flat[0:1, off + cg * 512:off + (cg + 1) * 512],
                              start=True, stop=True, inc=s_tm,
                              waits=[(s_tw, t_small), (s_ve, prev_ev)])
                        prev_ev = E_.vop("tensor_copy",
                                         dstb[:, cg * 512:(cg + 1) * 512],
                                         ps_tr[:], inc=s_ve,
                                         waits=[(s_tm, s_tm.v)])
                for c in range(24):
                    E_.mm(ps_tr[:, 0:1],
                          inbf[0:1, c * 128:(c + 1) * 128],
                          self.ones_f[0:1, 0:1], start=True, stop=True,
                          inc=s_tm,
                          waits=[(s_tw, t_small), (s_ve, prev_ev)])
                    prev_ev = E_.vop("tensor_copy", inb[:, c:c + 1],
                                     ps_tr[:, 0:1], inc=s_ve,
                                     waits=[(s_tm, s_tm.v)])
                t_bcast = prev_ev
                t_flatmm = s_tm.v
                for h in range(LH):
                    E_.dma("gpsimd", w1a.ap().rearrange(
                               "p (h a e) -> p h a e", h=LH, a=2)[:, h],
                           self.tw1[h].rearrange("(a p) e -> p a e", p=128),
                           s_tw, waits=[(s_tm, t_flatmm)])
                t_small = s_tw.v

                apl = [None] * NTT
                lnt = [None] * NTT
                prev_tr_evac = t_bcast
                lnt_mm = [None]

                def ln_tile(tt):
                    nonlocal prev_tr_evac
                    p = tt % 2
                    xld = E_.dma("sync", xst[p][:],
                                 src[tt * 128:(tt + 1) * 128, :], s_x[p],
                                 serial=True,
                                 waits=[(s_ve, apl[tt - 2])] if tt >= 2 else [])
                    t_ap1 = None
                    for (src_t, ridx) in ((xst[p], 0), (y_st, 1)):
                        E_.vop("bn_stats", stats[:, 0:6], src_t[:, 0:512],
                               waits=([(s_x[p], xld)] if ridx == 0
                                      else [(s_ve, t_ap1)]))
                        tbn = E_.vop("bn_stats", stats[:, 6:12],
                                     src_t[:, 512:1024], inc=s_ve)
                        tag_ = E_.vop("bn_aggr", mv[:], stats[:, 0:12],
                                      inc=s_ve, waits=[(s_ve, tbn)])
                        tve = E_.vop("tensor_single_scalar", veps[:],
                                     mv[:, 1:2], EPS, ALU.add, inc=s_ve,
                                     waits=[(s_ve, tag_)])
                        tln = E_.act(lnv[:], veps[:], AF.Ln, inc=s_se,
                                     waits=[(s_ve, tve)])
                        tsq = E_.act(rstd[:, ridx:ridx + 1], lnv[:], AF.Exp,
                                     scale=-0.5, inc=s_se,
                                     waits=[(s_se, tln)])
                        if ridx == 0:
                            t_ap1 = E_.vop(
                                "tensor_scalar", y_st[:], src_t[:],
                                mv[:, 0:1], rstd[:, 0:1],
                                op0=ALU.subtract, op1=ALU.mult, inc=s_ve,
                                waits=[(s_se, tsq), (s_ve, tag_)])
                        else:
                            tz = E_.vop("tensor_scalar", y_st[:], y_st[:],
                                        mv[:, 0:1], rstd[:, 1:2],
                                        op0=ALU.subtract, op1=ALU.mult,
                                        inc=s_ve,
                                        waits=[(s_se, tsq), (s_ve, tag_)])
                            tz = E_.vop("tensor_tensor", y_st[:], y_st[:],
                                        gbc[:], op=ALU.mult, inc=s_ve,
                                        waits=[(s_ve, tz)])
                            apl[tt] = E_.vop(
                                "tensor_tensor",
                                xtok[:, tt * D:(tt + 1) * D], y_st[:],
                                bbc[:], op=ALU.add, inc=s_ve,
                                waits=[(s_ve, tz)] +
                                      ([(s_tm, lnt_mm[0])]
                                       if tt >= 1 else []))
                    for dcg in range(2):
                        for dcl in range(4):
                            dc = dcg * 4 + dcl
                            E_.mm(ps_tr[:, dcl * 128:(dcl + 1) * 128],
                                  xtok[:, tt * D + dc * 128:
                                       tt * D + (dc + 1) * 128],
                                  self.ident[:], start=True, stop=True,
                                  inc=s_tm if dcl == 3 else None,
                                  waits=[(s_ve, apl[tt]),
                                         (s_ve, prev_tr_evac)])
                        tmm = s_tm.v
                        xv = xnT.ap().rearrange("p (dc t) -> p dc t", dc=DC)
                        prev_tr_evac = E_.vop(
                            "tensor_copy",
                            xv[:, dcg * 4:(dcg + 1) * 4,
                               tt * 128:(tt + 1) * 128],
                            ps_tr.ap().rearrange("p (a b) -> p a b", a=4),
                            inc=s_ve, waits=[(s_tm, tmm)])
                    lnt_mm[0] = s_tm.v
                    lnt[tt] = prev_tr_evac

                for tt in range(CCT):
                    ln_tile(tt)

                w1v = w1a.ap().rearrange("p (h a e) -> p h a e", h=LH, a=2)
                w0v = w0T.ap().rearrange("p (h a e) -> p h a e", h=LH, a=2)
                w2v = w2T.ap().rearrange("p (h a e) -> p h a e", h=LH, a=2)
                tw_tr = None
                for h in range(LH):
                    for (wsrc, wdstv) in ((self.tw0, w0v), (self.tw2, w2v)):
                        tn = E_.dma("gpsimd",
                                    twn.ap().rearrange("p (a e) -> p a e", a=2),
                                    wsrc[h].rearrange("(a p) d -> p a d",
                                                      p=128),
                                    s_tw, waits=[(s_tm, tw_tr)])
                        tloc = s_tw.v
                        for dc in range(2):
                            for ec in range(2):
                                E_.mm(ps_tr[:, (dc * 2 + ec) * 128:
                                            (dc * 2 + ec + 1) * 128],
                                      twn[:, ec * DIN + dc * 128:
                                          ec * DIN + (dc + 1) * 128],
                                      self.ident[:], start=True, stop=True,
                                      inc=(s_tm if dc == 1 and ec == 1
                                           else None),
                                      waits=[(s_tw, tloc),
                                             (s_ve, prev_tr_evac)])
                        tw_tr = s_tm.v
                        prev_tr_evac = E_.vop(
                            "tensor_copy", wdstv[:, h],
                            ps_tr.ap().rearrange("p (a e) -> p a e", a=2),
                            inc=s_ve, waits=[(s_tm, tw_tr)])
                t_twdone = prev_tr_evac

                lct_w = [[None] * 8 for _ in range(2)]
                lsu = [None, None]
                pfree = [t_twdone, t_twdone, t_twdone]
                tpre = t_twdone
                for n in range(NCK):
                    for h in range(LH):
                        if n == 0 and NCK > 1:
                            for tt2 in range(CCT + 2 * h,
                                             min(NTT, CCT + 2 * h + 2)):
                                ln_tile(tt2)

                        fams = ((w0v[:, h], gt_s, None, "gate", None),
                                (w2v[:, h], g_s, None, None, None),
                                (w1v[:, h], dh_s, -1.0, None, None))
                        for fi, (wap, outs, scl, sfn, souts) in enumerate(fams):
                            for ct in range(CCT):
                                pX = (pA, pB)[(fi * CCT + ct) % 2]
                                for dc in range(2):
                                    E_.mm(pX[:, 0:DIN],
                                          xnT[:, (h * 2 + dc) * T + n * CHK
                                              + ct * 128:
                                              (h * 2 + dc) * T + n * CHK
                                              + (ct + 1) * 128],
                                          wap[:, dc],
                                          start=(dc == 0), stop=(dc == 1),
                                          inc=s_tm if dc == 1 else None,
                                          waits=[(s_ve, lnt[n * CCT
                                                            + CCT - 1]),
                                                 (s_ve, pfree[(fi * CCT + ct)
                                                              % 2])])
                                tmm = s_tm.v
                                cs = slice(ct * DIN, (ct + 1) * DIN)
                                if sfn == "gate":
                                    E_.act(h_s[:, cs], pX[:, 0:DIN], AF.Silu,
                                           inc=s_se, waits=[(s_tm, tmm)])
                                    E_.act(sg_s[:, cs], pX[:, 0:DIN],
                                           AF.Sigmoid, inc=s_se)
                                ev = E_.vop(
                                    "tensor_copy", outs[:, cs], pX[:, 0:DIN],
                                    inc=s_ve,
                                    waits=[(s_tm, tmm)] +
                                          ([(s_se, s_se.v)]
                                           if sfn == "gate" else []))
                                if scl is not None:
                                    ev = E_.vop("tensor_single_scalar",
                                                outs[:, cs], outs[:, cs],
                                                -1.0, ALU.mult, inc=s_ve,
                                                waits=[(s_ve, ev)])
                                pfree[(fi * CCT + ct) % 2] = ev
                        W_ = slice(0, CCT * DIN)
                        tb0 = s_ve.v
                        c2 = E_.vop("tensor_tensor", hid_b[:, W_], h_s[:, W_],
                                    g_s[:, W_], op=ALU.mult, inc=s_ve,
                                    waits=[(s_ve, tb0), (s_se, s_se.v)])
                        c3 = E_.vop("tensor_tensor", dg_b[:, W_], dh_s[:, W_],
                                    h_s[:, W_], op=ALU.mult, inc=s_ve,
                                    waits=[(s_ve, c2)])
                        c4 = E_.vop("tensor_tensor", g_s[:, W_], dh_s[:, W_],
                                    g_s[:, W_], op=ALU.mult, inc=s_ve,
                                    waits=[(s_ve, c3)])
                        c5 = E_.vop("tensor_scalar", hq_s[:, W_], sg_s[:, W_],
                                    -1.0, 1.0, op0=ALU.mult, op1=ALU.add,
                                    inc=s_ve, waits=[(s_ve, tb0)])
                        c6 = E_.vop("tensor_tensor", hq_s[:, W_], h_s[:, W_],
                                    hq_s[:, W_], op=ALU.mult, inc=s_ve,
                                    waits=[(s_ve, c5)])
                        c7 = E_.vop("tensor_tensor", hq_s[:, W_], sg_s[:, W_],
                                    hq_s[:, W_], op=ALU.add, inc=s_ve,
                                    waits=[(s_ve, c6)])
                        c9 = E_.vop("tensor_tensor", dpre_b[:, W_], g_s[:, W_],
                                    hq_s[:, W_], op=ALU.mult, inc=s_ve,
                                    waits=[(s_ve, c4), (s_ve, c7)])
                        upds = ((hid_b, w1v, w1n, ALU.add, c2),
                                (dpre_b, w0v, w0nT, ALU.subtract, c9),
                                (dg_b, w2v, w2nT, ALU.subtract, c3))
                        for ui, (movb, wbase, outw, op, crdy) in \
                                enumerate(upds):
                            for dc2 in range(2):
                                pX = (pA, pB)[(ui * 2 + dc2) % 2]
                                for ct in range(CCT):
                                    E_.mm(pX[:, 0:DIN],
                                          xtok[:, (n * CCT + ct) * D
                                               + h * DIN + dc2 * 128:
                                               (n * CCT + ct) * D + h * DIN
                                               + (dc2 + 1) * 128],
                                          movb[:, ct * DIN:(ct + 1) * DIN],
                                          start=(ct == 0),
                                          stop=(ct == CCT - 1),
                                          inc=s_tm if ct == CCT - 1 else None,
                                          waits=[(s_ve, crdy),
                                                 (s_ve, pfree[(ui * 2 + dc2)
                                                              % 2])])
                                tmm = s_tm.v
                                ev = E_.vop(
                                    "tensor_tensor",
                                    outw[:, dc2 * DIN:(dc2 + 1) * DIN],
                                    wbase[:, h, dc2], pX[:, 0:DIN], op=op,
                                    inc=s_ve, waits=[(s_tm, tmm)])
                                pfree[(ui * 2 + dc2) % 2] = ev
                        tw1n = pfree[(2 * 2 + 1) % 2]
                        for ec in range(2):
                            for dc2 in range(2):
                                E_.mm(ps_tr[:, (ec * 2 + dc2) * 128:
                                            (ec * 2 + dc2 + 1) * 128],
                                      w1n[:, dc2 * DIN + ec * 128:
                                          dc2 * DIN + (ec + 1) * 128],
                                      self.ident[:], start=True, stop=True,
                                      inc=(s_tm if ec == 1 and dc2 == 1
                                           else None),
                                      waits=[(s_ve, pfree[0]),
                                             (s_ve, pfree[1]),
                                             (s_ve, tpre)])
                        tmm = s_tm.v
                        tpre = E_.vop(
                            "tensor_copy",
                            w1nT.ap().rearrange("p (a e) -> p a e", a=2),
                            ps_tr.ap().rearrange("p (a e) -> p a e", a=2),
                            inc=s_ve, waits=[(s_tm, tmm)])
                        banksD = [pA, pB, pC, ps_tr]
                        bfree = [(s_ve, pfree[0]), (s_ve, pfree[1]),
                                 (s_ve, pfree[2]), (s_ve, tpre)]
                        for ct in range(CCT):
                            cs = slice(ct * DIN, (ct + 1) * DIN)
                            bq = banksD[(2 * ct) % 4]
                            bq2 = banksD[(2 * ct + 1) % 4]
                            w0_ = bfree[(2 * ct) % 4]
                            w2_ = bfree[(2 * ct + 1) % 4]
                            for dc2 in range(2):
                                E_.mm(bq[:, 0:DIN],
                                      xnT[:, (h * 2 + dc2) * T + n * CHK
                                          + ct * 128:
                                          (h * 2 + dc2) * T + n * CHK
                                          + (ct + 1) * 128],
                                      w0nT[:, dc2 * DIN:(dc2 + 1) * DIN],
                                      start=(dc2 == 0), stop=(dc2 == 1),
                                      inc=s_tm if dc2 == 1 else None,
                                      waits=[w0_] if w0_ else [])
                            tgq = s_tm.v
                            for dc2 in range(2):
                                E_.mm(bq2[:, 0:DIN],
                                      xnT[:, (h * 2 + dc2) * T + n * CHK
                                          + ct * 128:
                                          (h * 2 + dc2) * T + n * CHK
                                          + (ct + 1) * 128],
                                      w2nT[:, dc2 * DIN:(dc2 + 1) * DIN],
                                      start=(dc2 == 0), stop=(dc2 == 1),
                                      inc=s_tm if dc2 == 1 else None,
                                      waits=[w2_] if w2_ else [])
                            tg2 = s_tm.v
                            tsg2 = E_.act(h_s[:, cs], bq[:, 0:DIN],
                                          AF.Silu, inc=s_se,
                                          waits=[(s_tm, tgq)])
                            bfree[(2 * ct) % 4] = (s_se, tsg2)
                            thg = E_.vop("tensor_tensor", hg_b[:, cs],
                                         h_s[:, cs], bq2[:, 0:DIN],
                                         op=ALU.mult, inc=s_ve,
                                         waits=[(s_se, tsg2), (s_tm, tg2)])
                            bfree[(2 * ct + 1) % 4] = (s_ve, thg)
                        thg_all = s_ve.v
                        trw = [w for w in bfree if w]
                        for ec in range(2):
                            for ctg in range(0, CCT, 4):
                                gw = min(4, CCT - ctg)
                                for ctl in range(gw):
                                    ct = ctg + ctl
                                    E_.mm(ps_tr[:, ctl * 128:(ctl + 1) * 128],
                                          hg_b[:, ct * DIN + ec * 128:
                                               ct * DIN + (ec + 1) * 128],
                                          self.ident[:], start=True,
                                          stop=True,
                                          inc=s_tm if ctl == gw - 1 else None,
                                          waits=[(s_ve, thg_all),
                                                 (s_ve, tpre)] + trw)
                                tmm = s_tm.v
                                tpre = E_.vop(
                                    "tensor_copy",
                                    hgT_b[:, ec * CHK + ctg * 128:
                                          ec * CHK + (ctg + gw) * 128],
                                    ps_tr[:, 0:gw * 128],
                                    inc=s_ve, waits=[(s_tm, tmm)])
                        for ct in range(CCT):
                            for ec in range(2):
                                E_.mm(pC[:, 0:DIN],
                                      hgT_b[:, ec * CHK + ct * 128:
                                            ec * CHK + (ct + 1) * 128],
                                      w1nT[:, ec * DIN:(ec + 1) * DIN],
                                      start=(ec == 0), stop=(ec == 1),
                                      inc=s_tm if ec == 1 else None,
                                      waits=[(s_ve, tpre),
                                             (s_ve, pfree[2]),
                                             (s_lc[ct % 2], lsu[ct % 2])]
                                            + trw)
                            tmm = s_tm.v
                            ev = E_.vop("tensor_copy", ls[ct % 2][:],
                                        pC[:, 0:DIN], inc=s_ve,
                                        waits=[(s_tm, tmm),
                                               (s_lc[ct % 2], lsu[ct % 2])])
                            pfree[2] = ev
                            lsu[ct % 2] = E_.dma(
                                "gpsimd",
                                self.lacts[n * CHK + ct * 128:
                                           n * CHK + (ct + 1) * 128,
                                           h * DIN:(h + 1) * DIN],
                                ls[ct % 2][:], s_lc[ct % 2],
                                waits=[(s_ve, ev)])
                ttt_tm = s_tm.v
                ttt_ve = s_ve.v
                ttt_se = s_se.v

                pj_tok = [None] * NP
                qk_done = [None] * NP
                pv_done = [None] * NP
                negb_tok = [None] * NP
                qkv_tok = [None] * NP
                ptu = [None, None]
                qb_free = ttt_ve
                pjn_free = None
                stu = [ttt_se, ttt_se]
                pv_free = [ttt_ve, ttt_ve]
                dn_free = [None, None]
                dn_par = [0]

                def proj_qkv_norms(p):
                    nonlocal qb_free, pjn_free
                    pp = p % 2
                    wts = [(s_tm, ttt_tm), (s_ve, ttt_ve), (s_se, ttt_se),
                           (s_tm, pjn_free)]
                    if p >= 2:
                        wts += [(s_tm, qk_done[p - 2]),
                                (s_tm, pv_done[p - 2])]
                    tpj = None
                    for m in range(3):
                        tpj = E_.dma(
                            "gpsimd", pjn[:, m * D:(m + 1) * D],
                            self.ipw[m * D + p * 128:m * D + (p + 1) * 128,
                                     :],
                            s_pj[pp], waits=wts)
                    pj_tok[p] = tpj
                    for dc in range(DC):
                        for m in range(3):
                            E_.mm(qkb[:, m * 128:(m + 1) * 128],
                                  pjn[:, m * D + dc * 128:
                                      m * D + (dc + 1) * 128],
                                  self.ident[:], start=True, stop=True,
                                  inc=s_tm if m == 2 else None,
                                  waits=[(s_pj[pp], tpj), (s_ve, qb_free),
                                         (s_tm, ttt_tm)])
                        tmm = s_tm.v
                        qb_free = E_.vop(
                            "tensor_copy", pjT[pp][:, dc * 384:dc * 384 + 384],
                            qkb[:, 0:384], inc=s_ve, waits=[(s_tm, tmm)])
                    pjn_free = s_tm.v
                    ev = None
                    for (mi, dstq, bias_c, scl) in ((0, Qp[pp], p, 0.125),
                                                    (1, Kp[pp], 8 + p, None)):
                        for tg in range(NG):
                            for dc in range(DC):
                                E_.mm(qkb[:],
                                      pjT[pp][:, dc * 384 + mi * 128:
                                              dc * 384 + (mi + 1) * 128],
                                      xnT[:, dc * T + tg * 512:
                                          dc * T + (tg + 1) * 512],
                                      start=(dc == 0), stop=(dc == DC - 1),
                                      inc=s_tm if dc == DC - 1 else None,
                                      waits=[(s_ve, qb_free),
                                             (s_se, ev),
                                             (s_tw, t_small)])
                            tmm = s_tm.v
                            ev = E_.act(dstq[:, tg * 512:(tg + 1) * 512],
                                        qkb[:], AF.Identity,
                                        bias=inb[:, bias_c:bias_c + 1],
                                        scale=scl, inc=s_se,
                                        waits=[(s_tm, tmm)])
                    tv4 = None
                    for r in range(4):
                        tv4 = E_.vop("tensor_copy",
                                     vbc4[:, r * 128:(r + 1) * 128],
                                     vbc[:, p * 128:(p + 1) * 128],
                                     inc=s_ve, waits=[(s_ve, t_bcast)])
                    evv = tv4
                    for t4 in range(0, NTT, 4):
                        for tl in range(4):
                            tt = t4 + tl
                            for dc in range(DC):
                                E_.mm(qkb[:, tl * 128:(tl + 1) * 128],
                                      xnT[:, dc * T + tt * 128:
                                          dc * T + (tt + 1) * 128],
                                      pjT[pp][:, dc * 384 + 256:
                                              dc * 384 + 384],
                                      start=(dc == 0), stop=(dc == DC - 1),
                                      inc=(s_tm if dc == DC - 1 and tl == 3
                                           else None),
                                      waits=[(s_ve, evv), (s_se, ev),
                                             (s_tm, pv_done[p - 2] if p >= 2
                                              else None)])
                        tmm = s_tm.v
                        evv = E_.vop(
                            "tensor_tensor",
                            Vp[pp][:, t4 * 128:(t4 + 4) * 128],
                            qkb[:], vbc4[:], op=ALU.add, inc=s_ve,
                            waits=[(s_tm, tmm), (s_ve, t_bcast)])
                    qb_free = evv
                    tq = E_.vop("tensor_tensor", sqb[:], Qp[pp][:], Qp[pp][:],
                                op=ALU.mult, inc=s_ve, waits=[(s_se, ev)])
                    toks = []
                    for (qi, srcb) in ((0, Qp[pp]), (1, Kp[pp])):
                        if qi == 1:
                            tq = E_.vop("tensor_tensor", sqb[:], srcb[:],
                                        srcb[:], op=ALU.mult, inc=s_ve,
                                        waits=[(s_ve, toks[-1]),
                                               (s_se, ev)])
                        for cg in range(NG):
                            E_.mm(qkb[0:2, 0:512], ones2[:],
                                  sqb[:, cg * 512:(cg + 1) * 512],
                                  start=True, stop=True, inc=s_tm,
                                  waits=[(s_ve, tq), (s_ve, qb_free)])
                            qb_free = E_.vop(
                                "reduce_max",
                                nrm[0:2, qi * NG + cg:qi * NG + cg + 1],
                                qkb[0:2, 0:512], axis=AX.X, inc=s_ve,
                                waits=[(s_tm, s_tm.v)])
                            toks.append(qb_free)
                    b1 = E_.vop("reduce_max", bnd[0:2, 0:1], nrm[0:2, 0:NG],
                                axis=AX.X, inc=s_ve, waits=[(s_ve, toks[-1])])
                    b2 = E_.vop("reduce_max", bnd[0:2, 1:2],
                                nrm[0:2, NG:2 * NG],
                                axis=AX.X, inc=s_ve, waits=[(s_ve, b1)])
                    b3 = E_.vop("tensor_tensor", bnd[0:2, 2:3], bnd[0:2, 0:1],
                                bnd[0:2, 1:2], op=ALU.mult, inc=s_ve,
                                waits=[(s_ve, b2)])
                    b4 = E_.act(bnd[0:2, 3:4], bnd[0:2, 2:3], AF.Ln,
                                inc=s_se, waits=[(s_ve, b3)])
                    b5 = E_.act(bnd[0:2, 4:5], bnd[0:2, 3:4], AF.Exp,
                                scale=0.5, inc=s_se, waits=[(s_se, b4)])
                    b6a = E_.vop("tensor_single_scalar", bnd[0:2, 6:7],
                                 bnd[0:2, 4:5], 0.5, ALU.mult, inc=s_ve,
                                 waits=[(s_se, b5)])
                    b6b = E_.vop("tensor_single_scalar", bnd[0:2, 7:8],
                                 bnd[0:2, 4:5], -80.0, ALU.add, inc=s_ve,
                                 waits=[(s_ve, b6a)])
                    b6c = E_.vop("tensor_tensor", bnd[0:2, 6:7],
                                 bnd[0:2, 6:7], bnd[0:2, 7:8], op=ALU.max,
                                 inc=s_ve, waits=[(s_ve, b6b)])
                    b6 = E_.vop("tensor_single_scalar", bnd[0:2, 5:6],
                                bnd[0:2, 6:7], -1.0, ALU.mult, inc=s_ve,
                                waits=[(s_ve, b6c)])
                    b7 = E_.vop("tensor_copy", bndb[:], bnd[0:2, 5:6],
                                inc=s_ve, waits=[(s_ve, b6)])
                    for hl in range(2):
                        E_.mm(qkb[:, 0:1],
                              selb[:, hl * 128:(hl + 1) * 128],
                              bndb[:], start=True, stop=True, inc=s_tm,
                              waits=[(s_ve, b7), (s_ve, qb_free)])
                        qb_free = E_.vop(
                            "tensor_copy", negb[:, 2 * p + hl:2 * p + hl + 1],
                            qkb[:, 0:1], inc=s_ve, waits=[(s_tm, s_tm.v)])
                    qb_free = E_.vop("tensor_tensor", negbp[:, p:p + 1],
                                     negb[:, 2 * p:2 * p + 1],
                                     negb[:, 2 * p + 1:2 * p + 2],
                                     op=ALU.min, inc=s_ve,
                                     waits=[(s_ve, qb_free)])
                    negb_tok[p] = qb_free
                    qkv_tok[p] = qb_free

                def attn(p):
                    nonlocal qb_free
                    pp = p % 2
                    dn_free[1] = (s_ve, max(qb_free or 0,
                                            (dn_free[1] or (None, 0))[1]))

                    def emit_st(q0, kt):
                        kp = kt % 2
                        E_.mm(stAB[kp][:, 0:512],
                              Kp[pp][0:64, kt * 128:(kt + 1) * 128],
                              Qp[pp][0:64, q0:q0 + 512],
                              start=True, stop=True, inc=s_tm,
                              waits=[(s_se, stu[kp]),
                                     (s_ve, qkv_tok[p]),
                                     (s_ve, negb_tok[p])])
                        return E_.mm(stAB[kp][:, 512:1024],
                                     Kp[pp][64:128,
                                            kt * 128:(kt + 1) * 128],
                                     Qp[pp][64:128, q0:q0 + 512],
                                     start=True, stop=True, inc=s_tm)

                    last_st = None
                    groups = list(range(0, T, 512))
                    st_tok = {0: emit_st(groups[0], 0)}
                    for gi, q0 in enumerate(groups):
                        gp = gi % 2
                        pvb = pv[gp]
                        dnb = (dn, qkb)[dn_par[0] % 2]
                        dnw = dn_free[dn_par[0] % 2]
                        for kt in range(NTT):
                            kp = kt % 2
                            tb = st_tok[kt]
                            ea = E_.act(PT[kp][:], stAB[kp][:], AF.Exp,
                                        bias=negbp[:, p:p + 1], inc=s_se,
                                        waits=[(s_tm, tb),
                                               (s_tm, ptu[kp])])
                            stu[kp] = ea
                            if kt + 1 < NTT:
                                st_tok[kt + 1] = emit_st(q0, kt + 1)
                            elif gi + 1 < len(groups):
                                st_tok[0] = emit_st(groups[gi + 1], 0)
                            E_.mm(pvb[0:64, :],
                                  Vp[pp][:, kt * 128:kt * 128 + 64],
                                  PT[kp][:, 0:512],
                                  start=(kt == 0), stop=(kt == NTT - 1),
                                  waits=[(s_se, ea), (s_ve, pv_free[gp])])
                            E_.mm(pvb[64:128, :],
                                  Vp[pp][:, kt * 128 + 64:kt * 128 + 128],
                                  PT[kp][:, 512:1024],
                                  start=(kt == 0), stop=(kt == NTT - 1),
                                  skip_group_check=True,
                                  waits=[(s_se, ea)])
                            E_.mm(dnb[0:2, :], onz[:], PT[kp][:, 0:512],
                                  start=(kt == 0), stop=False,
                                  inc=None, skip_group_check=True,
                                  waits=[dnw] if dnw else [])
                            E_.mm(dnb[0:2, :], zno[:], PT[kp][:, 512:1024],
                                  start=False, stop=(kt == NTT - 1),
                                  inc=s_tm, skip_group_check=True)
                            ptu[kp] = s_tm.v
                            last_st = s_tm.v
                        r2 = E_.vop("reciprocal", rr2[0:2, :],
                                    dnb[0:2, :], inc=s_ve,
                                    waits=[(s_tm, last_st)])
                        E_.mm(dnb[0:128, :], self_f[:], rr2[:],
                              start=True, stop=True, inc=s_tm,
                              waits=[(s_ve, r2)], skip_group_check=True)
                        tbc = E_.vop("tensor_copy", rbc[:], dnb[0:128, :],
                                     inc=s_ve, waits=[(s_tm, s_tm.v)])
                        dn_free[dn_par[0] % 2] = (s_ve, tbc)
                        dn_par[0] += 1
                        pv_free[gp] = E_.vop(
                            "tensor_tensor",
                            aoT[:, p * T + q0:p * T + q0 + 512],
                            pvb[:], rbc[:], op=ALU.mult, inc=s_ve,
                            waits=[(s_ve, tbc), (s_tm, last_st)])
                    qk_done[p] = last_st
                    pv_done[p] = last_st
                    qb_free = max(qb_free or 0, (dn_free[1] or (None, 0))[1])

                proj_qkv_norms(0)
                for p in range(NP):
                    if p + 1 < NP:
                        proj_qkv_norms(p + 1)
                    attn(p)
                ao_done = s_ve.v
                if DEBUG:
                    dsem = self.sem("lc0")
                    E_.dma("gpsimd", self.dbg[:, 0:16], negb[:],
                           dsem, waits=[(s_ve, ao_done)])
                    E_.dma("gpsimd", self.dbg[0:2, 16:32], nrm[0:2, 0:16],
                           dsem)
                    E_.dma("gpsimd", self.dbg[0:2, 32:40], bnd[0:2, :], dsem)
                    E_.dma("gpsimd", self.dbg[:, 64:2112], Qp[0][:], dsem)
                    E_.dma("gpsimd", self.dbg[:, 2112:4160], Kp[0][:], dsem)
                    E_.dma("gpsimd", self.dbg[:, 4160:6208], Vp[0][:, 0:2048], dsem)
                    E_.dma("gpsimd", self.dbg[:, 6208:22592],
                           aoT[:], dsem)
                    E_.dma("gpsimd", self.dbg[:, 22592:25664],
                           pjT[0][:], dsem)

                t_all_attn = s_tm.v
                tr_tok = [None] * DC
                opv = opT.ap().rearrange("p (dc d) -> p dc d", dc=DC)
                for c in range(DC):
                    tpo = E_.dma(
                        "gpsimd", pjn[:, (c % 2) * D:(c % 2 + 1) * D],
                        self.opw[c * 128:(c + 1) * 128, :], s_pj[c % 2],
                        waits=[(s_tm, t_all_attn if c < 2
                                else tr_tok[c - 2])])
                    for dcg in range(2):
                        for dcl in range(4):
                            dc = dcg * 4 + dcl
                            E_.mm(qkb[:, dcl * 128:(dcl + 1) * 128],
                                  pjn[:, (c % 2) * D + dc * 128:
                                      (c % 2) * D + (dc + 1) * 128],
                                  self.ident[:], start=True, stop=True,
                                  inc=s_tm if dcl == 3 else None,
                                  waits=[(s_pj[c % 2], tpo),
                                         (s_ve, qb_free)])
                        tr_tok[c] = s_tm.v
                        qb_free = E_.vop(
                            "tensor_copy",
                            opv[:, dcg * 4:(dcg + 1) * 4,
                                c * 128:(c + 1) * 128],
                            qkb.ap().rearrange("p (a b) -> p a b", a=4),
                            inc=s_ve, waits=[(s_tm, tr_tok[c])])
                opT_rdy = qb_free

                lr_tok = [None, None]
                x1_tok = [None, None]
                ev_prev = [None, None]
                x2_tok = [None, None]
                po_tok = None
                for tt in range(NTT):
                    pp = tt % 2
                    x1_tok[pp] = E_.dma(
                        "sync", x1t[pp][:],
                        src[tt * 128:(tt + 1) * 128, :], s_x[pp],
                        serial=True,
                        waits=[(s_ve, ev_prev[pp]), (s_tm, ttt_tm),
                               (s_ve, ttt_ve), (s_se, ttt_se)])
                    lr_tok[pp] = E_.dma(
                        "gpsimd", lct[pp][:],
                        self.lacts[tt * 128:(tt + 1) * 128, :], s_lr[pp],
                        waits=[(s_lc[0], s_lc[0].v), (s_lc[1], s_lc[1].v),
                               (s_ve, ev_prev[pp]), (s_tm, ttt_tm),
                               (s_ve, ttt_ve), (s_se, ttt_se)])
                    for dh in range(2):
                        pso = stAB[0] if dh == 0 else stAB[1]
                        pso = pso.ap()[:, 0:512] if False else pso[:, 0:512]
                        for dc in range(DC):
                            E_.mm(pso[:],
                                  aoT[:, dc * T + tt * 128:
                                      dc * T + (tt + 1) * 128],
                                  opT[:, dc * D + dh * 512:
                                      dc * D + (dh + 1) * 512],
                                  start=(dc == 0), stop=(dc == DC - 1),
                                  inc=s_tm if dc == DC - 1 else None,
                                  waits=[(s_ve, opT_rdy),
                                         (s_ve, ao_done),
                                         (s_ve, ev_prev[pp]),
                                         (s_se, s_se.v)])
                        tmm = s_tm.v
                        t1 = E_.vop("tensor_tensor",
                                    x2t[pp][:, dh * 512:(dh + 1) * 512],
                                    pso[:], obc[:, dh * 512:(dh + 1) * 512],
                                    op=ALU.add, inc=s_ve,
                                    waits=[(s_tm, tmm), (s_ve, t_bcast),
                                           (s_xe[pp], x2_tok[pp])])
                        t2 = E_.vop("tensor_tensor",
                                    x2t[pp][:, dh * 512:(dh + 1) * 512],
                                    x2t[pp][:, dh * 512:(dh + 1) * 512],
                                    lct[pp][:, dh * 512:(dh + 1) * 512],
                                    op=ALU.add, inc=s_ve,
                                    waits=[(s_ve, t1),
                                           (s_lr[pp], lr_tok[pp])])
                        ev_prev[pp] = E_.vop(
                            "tensor_tensor",
                            x2t[pp][:, dh * 512:(dh + 1) * 512],
                            x2t[pp][:, dh * 512:(dh + 1) * 512],
                            x1t[pp][:, dh * 512:(dh + 1) * 512],
                            op=ALU.add, inc=s_ve,
                            waits=[(s_ve, t2), (s_x[pp], x1_tok[pp])])
                    x2_tok[pp] = E_.dma(
                        "gpsimd", dst[tt * 128:(tt + 1) * 128, :],
                        x2t[pp][:], s_xe[pp], waits=[(s_ve, ev_prev[pp])])

                E_.wait_on("gpsimd", [(s_xe[0], s_xe[0].v),
                                      (s_xe[1], s_xe[1].v),
                                      (s_lr[0], s_lr[0].v),
                                      (s_lr[1], s_lr[1].v),
                                      (s_lc[0], s_lc[0].v),
                                      (s_lc[1], s_lc[1].v),
                                      (s_pj[0], s_pj[0].v),
                                      (s_pj[1], s_pj[1].v),
                                      (s_tw, s_tw.v)])
                E_.wait_on("sync", [(s_x[0], s_x[0].v),
                                    (s_x[1], s_x[1].v)])

            E_.run("attn_blk", gen)
            st3.close()
            pst3.close()


def _mk_selb():
    s = np.zeros((2, 256), dtype=ml_dtypes.bfloat16)
    s[0, 0:128] = 1
    s[1, 128:256] = 1
    return s


def _mk_self():
    s = np.zeros((2, 128), dtype=np.float32)
    s[0, 0:64] = 1
    s[1, 64:128] = 1
    return s


_BUILT = None


def kernel(L0_attn_x, L0_attn_ln_w, L0_attn_ln_b, L0_attn_in_proj_w,
           L0_attn_in_proj_b, L0_attn_out_proj_w, L0_attn_out_proj_b,
           L0_attn_w0, L0_attn_w1, L0_attn_w2,
           L0_ffn_Wg, L0_ffn_W1, L0_ffn_W2, L0_ffn_W3):
    x = np.ascontiguousarray(np.asarray(L0_attn_x, dtype=np.float32))
    B = x.shape[0]
    nc = Builder().build()
    common = {
        "ffn_wg": np.ascontiguousarray(L0_ffn_Wg, np.float32),
        "ffn_w1": np.ascontiguousarray(L0_ffn_W1, np.float32),
        "ffn_w2": np.ascontiguousarray(L0_ffn_W2, np.float32),
        "ffn_w3": np.ascontiguousarray(L0_ffn_W3, np.float32),
        "ln_w": np.ascontiguousarray(L0_attn_ln_w, np.float32),
        "ln_b": np.ascontiguousarray(L0_attn_ln_b, np.float32),
        "in_proj_w": np.ascontiguousarray(L0_attn_in_proj_w, np.float32),
        "in_proj_b": np.ascontiguousarray(L0_attn_in_proj_b, np.float32),
        "out_proj_w": np.ascontiguousarray(L0_attn_out_proj_w, np.float32),
        "out_proj_b": np.ascontiguousarray(L0_attn_out_proj_b, np.float32),
        "ttt_w0": np.ascontiguousarray(L0_attn_w0, np.float32),
        "ttt_w1": np.ascontiguousarray(L0_attn_w1, np.float32),
        "ttt_w2": np.ascontiguousarray(L0_attn_w2, np.float32),
        "identD": np.eye(128, dtype=ml_dtypes.bfloat16),
        "selbD": _mk_selb(),
        "selfD": _mk_self(),
    }
    in_maps = [dict(common, x=np.ascontiguousarray(x[b])) for b in range(B)]
    res = run_bass_kernel_spmd(nc, in_maps, core_ids=list(range(B)))
    out = np.stack([np.asarray(r["y"], np.float32) for r in res.results], 0)
    return out


# version 11
# speedup vs baseline: 1.0258x; 1.0258x over previous
import contextlib
import numpy as np
import ml_dtypes
import concourse.bass as bass
import concourse.mybir as mybir
from concourse.bass_utils import run_bass_kernel_spmd

F32 = mybir.dt.float32
BF16 = mybir.dt.bfloat16
AF = mybir.ActivationFunctionType
ALU = mybir.AluOpType
AX = mybir.AxisListType

T = 2048
D = 1024
DC = D // 128
NTT = T // 128
E = 8
FF = 2048
FH = FF // 2
NSUB = 2 * E
FS = FH // 128
NG = T // 512
NH = 16
NP = NH // 2
HD = 64
LH = 4
DIN = 256
NCK = 2
CHK = 1024
CCT = 8
EPS = 1e-5
BIG = 1e30

PHASES = (1, 2, 3)
DEBUG = False
ENGINES = ("sync", "gpsimd", "tensor", "vector", "scalar")


class Cnt:
    __slots__ = ("sem", "v")
    def __init__(self, sem):
        self.sem = sem
        self.v = 0


class Emit:
    def __init__(self, nc, sems):
        self.nc = nc
        self.sems = sems
        self.cur = None
        self.waited = {}

    def run(self, name, gen):
        snap = {n: c.v for n, c in self.sems.items()}
        with self.nc.Block(name) as blk:
            for en in ENGINES:
                for n, c in self.sems.items():
                    c.v = snap[n]
                self.cur = en
                self.waited = {}
                getattr(blk, en)(lambda _e, g=gen: g())
        self.cur = None

    def _wait(self, waits):
        eng = getattr(self.nc, self.cur)
        for c, v in waits:
            if v is None or v <= 0:
                continue
            k = c.sem.num
            if self.waited.get(k, -1) >= v:
                continue
            eng.wait_ge(c.sem, v)
            self.waited[k] = v

    def _fin(self, i, inc):
        if inc is None:
            return None
        if isinstance(inc, tuple):
            c, n = inc
        else:
            c, n = inc, 1
        if i is not None:
            i.then_inc(c.sem, n)
        c.v += n
        return c.v

    def _go(self, en, waits):
        if self.cur != en:
            return False
        self._wait(waits)
        return True

    def dma(self, q, out, in_, sem, waits=(), accum=False, serial=False):
        i = None
        if serial:
            waits = list(waits) + [(sem, sem.v)]
        if self._go(q, waits):
            kw = {}
            if accum:
                kw["accum_op"] = ALU.add
            i = getattr(self.nc, q).dma_start(out=out, in_=in_, **kw)
        return self._fin(i, (sem, 16))

    def mm(self, out, lhsT, rhs, start, stop, inc=None, waits=(), **kw):
        i = None
        if self._go("tensor", waits):
            i = self.nc.tensor.matmul(out, lhsT, rhs, start=start, stop=stop,
                                      **kw)
        return self._fin(i, inc)

    def act(self, out, in_, func, bias=None, scale=None, accum_out=None,
            inc=None, waits=()):
        i = None
        if self._go("scalar", waits):
            kw = {}
            if bias is not None:
                kw["bias"] = bias
            if scale is not None:
                kw["scale"] = scale
            if accum_out is not None:
                kw["accum_out"] = accum_out
            i = self.nc.scalar.activation(out, in_, func, **kw)
        return self._fin(i, inc)

    def vop(self, fn, *args, inc=None, waits=(), **kw):
        i = None
        if self._go("vector", waits):
            i = getattr(self.nc.vector, fn)(*args, **kw)
        return self._fin(i, inc)

    def wait_on(self, en, waits):
        if self.cur == en:
            self._wait(waits)


class Builder:
    def __init__(self):
        nc = bass.Bass("TRN2", target_bir_lowering=False)
        self.nc = nc
        dt_ = nc.dram_tensor
        self.x = dt_("x", [T, D], F32, kind="ExternalInput").ap()
        self.Wg = dt_("ffn_wg", [E, D], F32, kind="ExternalInput").ap()
        self.W1 = dt_("ffn_w1", [E, FF, D], F32, kind="ExternalInput").ap()
        self.W2 = dt_("ffn_w2", [E, FF, D], F32, kind="ExternalInput").ap()
        self.W3 = dt_("ffn_w3", [E, D, FF], F32, kind="ExternalInput").ap()
        self.ln_w = dt_("ln_w", [D], F32, kind="ExternalInput").ap()
        self.ln_b = dt_("ln_b", [D], F32, kind="ExternalInput").ap()
        self.ipw = dt_("in_proj_w", [3 * D, D], F32, kind="ExternalInput").ap()
        self.ipb = dt_("in_proj_b", [3 * D], F32, kind="ExternalInput").ap()
        self.opw = dt_("out_proj_w", [D, D], F32, kind="ExternalInput").ap()
        self.opb = dt_("out_proj_b", [D], F32, kind="ExternalInput").ap()
        self.tw0 = dt_("ttt_w0", [LH, DIN, DIN], F32, kind="ExternalInput").ap()
        self.tw1 = dt_("ttt_w1", [LH, DIN, DIN], F32, kind="ExternalInput").ap()
        self.tw2 = dt_("ttt_w2", [LH, DIN, DIN], F32, kind="ExternalInput").ap()
        self.identD = dt_("identD", [128, 128], BF16, kind="ExternalInput").ap()
        self.selbD = dt_("selbD", [2, 256], BF16, kind="ExternalInput").ap()
        self.selfD = dt_("selfD", [2, 128], F32, kind="ExternalInput").ap()
        self.y = dt_("y", [T, D], F32, kind="ExternalOutput").ap()
        kk = "ExternalOutput" if DEBUG else "Internal"
        self.acc = dt_("acc_scr", [T, D], F32, kind=kk).ap()
        self.x2s = dt_("x2_scr", [T, D], F32, kind=kk).ap()
        self.lacts = dt_("lact_scr", [T, D], F32, kind=kk).ap()
        self.dbg = dt_("dbg", [128, 28 * 1024], F32,
                       kind="ExternalOutput" if DEBUG else "Internal").ap()
        self.w1s = dt_("w1t_scr", [E, 2, 128, FS * D], BF16, kind="Internal").ap()
        self.w2s = dt_("w2t_scr", [E, 2, 128, FS * D], BF16, kind="Internal").ap()
        self.w3s = dt_("w3t_scr", [E, 2, 128, FS * D], BF16, kind="Internal").ap()

        self.es = contextlib.ExitStack()
        self.sems = {}

    def sem(self, name):
        if name not in self.sems:
            self.sems[name] = Cnt(self.es.enter_context(self.nc.semaphore(name)))
        return self.sems[name]

    def sb(self, name, shape, dtype, es=None):
        return (es or self.es).enter_context(self.nc.sbuf_tensor(name, shape, dtype))

    def psum(self, name, shape, es=None):
        return (es or self.es).enter_context(self.nc.psum_tensor(name, shape, F32))

    def build(self):
        nc = self.nc
        with self.es:
            self.ident = self.sb("ident", [128, 128], BF16)
            self.ones_f = self.sb("ones_f", [1, 128], F32)
            names = (["xld0", "xld1", "nat", "nat1", "nat2", "nat3", "nat4",
                      "nat5", "wld0", "wld1", "scw0", "scw1", "init",
                      "tm", "ve", "se", "wgld", "pj0", "pj1", "xe0", "xe1",
                      "lctw", "ttw"]
                     + [f"rmw{i}" for i in range(NTT)])
            for n in names:
                self.sem(n)
            lo = min(c.sem.num for c in self.sems.values())
            hi = max(c.sem.num for c in self.sems.values())
            nc.gpsimd.dma_reset(range(lo, hi + 1))
            nc.gpsimd.sem_clear(range(lo, hi + 1))
            nc.all_engine_barrier()

            E_ = Emit(nc, self.sems)

            def prolog():
                t = E_.dma("gpsimd", self.ident[:], self.identD,
                           self.sem("wgld"))
                E_.vop("memset", self.ones_f[:], 1.0, inc=self.sem("ve"))
                E_.wait_on("gpsimd", [(self.sem("wgld"), t)])
            E_.run("prolog", prolog)

            if 1 in PHASES:
                self.ffn_phase(E_, 1, src=self.x, dst=self.acc, build_scratch=True)
            if 2 in PHASES:
                self.attn_phase(E_, src=self.acc if 1 in PHASES else self.x,
                                dst=self.x2s)
            if 3 in PHASES:
                self.ffn_phase(E_, 3, src=self.x2s if 2 in PHASES else self.x,
                               dst=self.y,
                               build_scratch=1 not in PHASES)
            else:
                src = self.x2s if 2 in PHASES else self.acc

                def copyout():
                    fin = self.sem("init")
                    t = E_.dma("sync", self.y, src, fin)
                    E_.wait_on("sync", [(fin, t)])
                E_.run("copyout", copyout)
        return nc

    def ffn_phase(self, E_, pidx, src, dst, build_scratch):
        nc = self.nc
        tag = f"f{pidx}_"
        st = contextlib.ExitStack()
        with st:
            sb = lambda n, sh, dt: self.sb(tag + n, sh, dt, st)
            psl = lambda n, sh: self.psum(tag + n, sh, st)

            xnT = sb("xnT", [128, DC * T], BF16)
            w1b = [sb(f"w1b{i}", [128, FS * D], BF16) for i in range(2)]
            w2b = [sb(f"w2b{i}", [128, FS * D], BF16) for i in range(2)]
            w3b = [sb(f"w3b{i}", [128, FS * D], BF16) for i in range(2)]
            hid = [sb(f"hid{i}", [128, FS * 512], BF16) for i in range(2)]
            asb = [sb("asb0", [128, 512], BF16)]
            tmp = [sb(f"tmp{i}", [128, D], F32) for i in range(2)]
            xst = [sb(f"xst{i}", [128, D], F32) for i in range(2)]
            xns = sb("xns", [128, D], BF16)
            nat = ([sb(f"nat{chr(65 + i)}", [128, 2048], BF16)
                    for i in range(6)] if build_scratch else None)
            wgb = sb("wgb", [E, D], BF16)
            wgT = sb("wgT", [128, DC * E], BF16)
            gates = sb("gates", [128, E * NTT], F32)
            stats = sb("stats", [128, 12], F32)
            mv = sb("mv", [128, 2], F32)
            veps = sb("veps", [128, 1], F32)
            lnv = sb("lnv", [128, 1], F32)
            rstd = sb("rstd", [128, NTT], F32)
            sc_all = sb("sc", [128, NTT * E], F32)
            m1a = sb("m1a", [128, NTT * E], F32)
            g8 = sb("g8", [128, E], F32)
            g8b = sb("g8b", [128, E], F32)
            gsc = sb("gsc", [128, 128], F32)

            ps_h1 = psl("h1", [128, 512])
            ps_h2 = psl("h2", [128, 512])
            ps_oa = psl("oa", [128, D])
            ps_ob = psl("ob", [128, D])
            ps_tr = psl("tr", [128, 512])
            ps_aux = psl("aux", [128, 512])
            ps_g = ps_aux

            s_x = [self.sem("xld0"), self.sem("xld1")]
            s_wld = [self.sem("wld0"), self.sem("wld1")]
            s_nat = [self.sem("nat"), self.sem("nat1"), self.sem("nat2"),
                     self.sem("nat3"), self.sem("nat4"), self.sem("nat5")]
            s_scw = [self.sem("scw0"), self.sem("scw1")]
            s_init = self.sem("init")
            s_wg = self.sem("wgld")
            s_tm = self.sem("tm")
            s_ve = self.sem("ve")
            s_se = self.sem("se")
            s_rmw = [self.sem(f"rmw{i}") for i in range(NTT)]

            def gen():
                t_init = E_.dma("sync", dst, src, s_init)
                t_wgl = E_.dma("gpsimd", wgb[:], self.Wg, s_wg)
                for dc in range(DC):
                    E_.mm(ps_tr[:, dc * E:(dc + 1) * E],
                          wgb[0:E, dc * 128:(dc + 1) * 128],
                          self.ident[0:E, 0:E], start=True, stop=True,
                          inc=s_tm if dc == DC - 1 else None,
                          waits=[(s_wg, t_wgl)])
                t_wgt = s_tm.v
                t_wge = E_.vop("tensor_copy", wgT[:], ps_tr[:, 0:DC * E],
                               inc=s_ve, waits=[(s_tm, t_wgt)])

                xld = [None] * NTT
                apl = [None] * NTT
                lnt = [None] * NTT
                scdone = [None] * NTT
                prev_tr_evac = t_wge
                prev_g_evac = None
                for tt in range(NTT):
                    p = tt % 2
                    xld[tt] = E_.dma(
                        "sync", xst[p][:], src[tt * 128:(tt + 1) * 128, :],
                        s_x[p], serial=True,
                        waits=[(s_ve, apl[tt - 2])] if tt >= 2 else [])
                    E_.vop("bn_stats", stats[:, 0:6], xst[p][:, 0:512],
                           waits=[(s_x[p], xld[tt])])
                    tbn = E_.vop("bn_stats", stats[:, 6:12],
                                 xst[p][:, 512:1024], inc=s_ve)
                    tag_ = E_.vop("bn_aggr", mv[:], stats[:, 0:12], inc=s_ve,
                                  waits=[(s_ve, tbn)])
                    tve = E_.vop("tensor_single_scalar", veps[:], mv[:, 1:2],
                                 EPS, ALU.add, inc=s_ve,
                                 waits=[(s_ve, tag_)])
                    tln = E_.act(lnv[:], veps[:], AF.Ln, inc=s_se,
                                 waits=[(s_ve, tve)])
                    tsq = E_.act(rstd[:, tt:tt + 1], lnv[:], AF.Exp,
                                 scale=-0.5, inc=s_se, waits=[(s_se, tln)])
                    apl[tt] = E_.vop(
                        "tensor_scalar", xns[:], xst[p][:],
                        mv[:, 0:1], rstd[:, tt:tt + 1],
                        op0=ALU.subtract, op1=ALU.mult, inc=s_ve,
                        waits=[(s_se, tsq), (s_ve, tag_)] +
                              ([(s_tm, lnt_mm)] if tt >= 1 else []))
                    for dcg in range(2):
                        for dcl in range(4):
                            dc = dcg * 4 + dcl
                            E_.mm(ps_tr[:, dcl * 128:(dcl + 1) * 128],
                                  xns[:, dc * 128:(dc + 1) * 128],
                                  self.ident[:], start=True, stop=True,
                                  inc=s_tm if dcl == 3 else None,
                                  waits=[(s_ve, apl[tt]),
                                         (s_ve, prev_tr_evac)])
                        tmm = s_tm.v
                        xv = xnT.ap().rearrange("p (dc t) -> p dc t", dc=DC)
                        prev_tr_evac = E_.vop(
                            "tensor_copy",
                            xv[:, dcg * 4:(dcg + 1) * 4,
                               tt * 128:(tt + 1) * 128],
                            ps_tr.ap().rearrange("p (a b) -> p a b", a=4),
                            inc=s_ve, waits=[(s_tm, tmm)])
                    lnt_mm = s_tm.v
                    lnt[tt] = prev_tr_evac
                    for dc in range(DC):
                        E_.mm(ps_g[:, 0:E],
                              xnT[:, dc * T + tt * 128:
                                  dc * T + (tt + 1) * 128],
                              wgT[:, dc * E:(dc + 1) * E],
                              start=(dc == 0), stop=(dc == DC - 1),
                              inc=s_tm if dc == DC - 1 else None,
                              waits=[(s_ve, t_wge), (s_ve, prev_g_evac),
                                     (s_ve, lnt[tt])])
                    tgs = s_tm.v
                    prev_g_evac = E_.vop(
                        "tensor_copy", sc_all[:, tt * E:(tt + 1) * E],
                        ps_g[:, 0:E], inc=s_ve, waits=[(s_tm, tgs)])
                    t1 = E_.vop("reduce_max", gsc[:, tt:tt + 1],
                                sc_all[:, tt * E:(tt + 1) * E], axis=AX.X,
                                inc=s_ve, waits=[(s_ve, prev_g_evac)])
                    t2 = E_.vop("tensor_scalar", m1a[:, tt * E:(tt + 1) * E],
                                sc_all[:, tt * E:(tt + 1) * E],
                                gsc[:, tt:tt + 1], None, op0=ALU.is_ge,
                                inc=s_ve, waits=[(s_ve, t1)])
                    t3 = E_.vop("scalar_tensor_tensor", g8[:],
                                m1a[:, tt * E:(tt + 1) * E], -BIG,
                                sc_all[:, tt * E:(tt + 1) * E],
                                op0=ALU.mult, op1=ALU.add,
                                inc=s_ve, waits=[(s_ve, t2)])
                    E_.vop("reduce_max", gsc[:, NTT + tt:NTT + 1 + tt], g8[:],
                           axis=AX.X, inc=s_ve, waits=[(s_ve, t3)])

                tdd = E_.vop("tensor_tensor", gsc[:, 2*NTT:3*NTT], gsc[:, 0:NTT],
                             gsc[:, NTT:2*NTT], op=ALU.subtract, inc=s_ve,
                             waits=[(s_ve, s_ve.v)])
                tsg = E_.act(gsc[:, 3*NTT:4*NTT], gsc[:, 2*NTT:3*NTT], AF.Sigmoid,
                             inc=s_se, waits=[(s_ve, tdd)])
                t4 = E_.vop("tensor_single_scalar", gsc[:, 4*NTT:5*NTT],
                            gsc[:, 3*NTT:4*NTT], 0.5, ALU.mult, inc=s_ve,
                            waits=[(s_se, tsg)])
                t5 = E_.vop("tensor_scalar", gsc[:, 5*NTT:6*NTT], gsc[:, 4*NTT:5*NTT],
                            -1.0, 0.5, op0=ALU.mult, op1=ALU.add, inc=s_ve,
                            waits=[(s_ve, t4)])
                t6 = E_.vop("tensor_tensor", gsc[:, 6*NTT:7*NTT], gsc[:, 4*NTT:5*NTT],
                            gsc[:, 5*NTT:6*NTT], op=ALU.subtract, inc=s_ve,
                            waits=[(s_ve, t5)])
                gv = gates.ap().rearrange("p (e t) -> p e t", e=E)
                t_gt = None
                tprev = t6
                for tt in range(NTT):
                    ta = E_.vop("tensor_scalar", g8[:],
                                sc_all[:, tt * E:(tt + 1) * E],
                                gsc[:, NTT + tt:NTT + 1 + tt], None, op0=ALU.is_ge,
                                inc=s_ve, waits=[(s_ve, tprev)])
                    tb = E_.vop("tensor_scalar", g8b[:], g8[:],
                                gsc[:, 5*NTT + tt:5*NTT + 1 + tt], None, op0=ALU.mult,
                                inc=s_ve, waits=[(s_ve, ta)])
                    t_gt = E_.vop("scalar_tensor_tensor",
                                  gv[:, :, tt:tt + 1],
                                  m1a[:, tt * E:(tt + 1) * E],
                                  gsc[:, 6*NTT + tt:6*NTT + 1 + tt], g8b[:],
                                  op0=ALU.mult, op1=ALU.add, inc=s_ve,
                                  waits=[(s_ve, tb)])
                    tprev = t_gt

                wready = [None] * NSUB
                wsem = [None] * NSUB
                sub_done = [None] * NSUB
                scw_tok = [None] * NSUB
                trv_tok = [None] * NSUB
                prev_sil = None
                prev_htt = None
                prev_h1mm = None
                prev_h2mm = None
                hid_user = [None, None]
                oev_last = [None, None]
                rmw_t = [None] * NTT
                prev_tr_mm = lnt_mm

                nat_tr = [None] * 6
                nat_ctr = [0]
                nat_tok = {}

                def weights_dma(k):
                    e, fh = k // 2, k % 2
                    p = k % 2
                    if not build_scratch:
                        wt = None
                        for wsrc, wdst in ((self.w1s, w1b), (self.w2s, w2b),
                                           (self.w3s, w3b)):
                            wt = E_.dma(
                                "sync", wdst[p][:], wsrc[e, fh], s_wld[p],
                                waits=([(s_tm, sub_done[k - 2])]
                                       if k >= 2 else []))
                        wready[k] = wt
                        wsem[k] = s_wld[p]
                        return
                    for mi, wsrcap in enumerate((
                            self.W1[e, fh * FH:(fh + 1) * FH, :]
                            .rearrange("(a p) d -> p a d", p=128),
                            self.W2[e, fh * FH:(fh + 1) * FH, :]
                            .rearrange("(a p) d -> p a d", p=128),
                            self.W3[e, :, fh * FH:(fh + 1) * FH]
                            .rearrange("(a p) f -> p a f", p=128))):
                        n_in = DC if mi == 2 else FS
                        wid = FH if mi == 2 else D
                        pieces = 4 if n_in >= 4 else 2
                        pb = n_in // pieces
                        for pc in range(pieces):
                            nb = nat_ctr[0] % 6
                            nat_ctr[0] += 1
                            nat_tok[(k, mi, pc)] = (nb, E_.dma(
                                "gpsimd",
                                nat[nb].ap()[:, 0:pb * wid].rearrange(
                                    "p (a d) -> p a d", a=pb),
                                wsrcap[:, pc * pb:(pc + 1) * pb, :],
                                s_nat[nb], waits=[(s_tm, nat_tr[nb])]))

                def weights_gen(k):
                    nonlocal prev_tr_evac, prev_tr_mm
                    e, fh = k // 2, k % 2
                    p = k % 2
                    if not build_scratch:
                        return
                    plist = []
                    wdsts = (w1b, w2b, w3b)
                    wscrs = (self.w1s, self.w2s, self.w3s)
                    srcs = (self.W1[e, fh * FH:(fh + 1) * FH, :]
                            .rearrange("(a p) d -> p a d", p=128),
                            self.W2[e, fh * FH:(fh + 1) * FH, :]
                            .rearrange("(a p) d -> p a d", p=128),
                            self.W3[e, :, fh * FH:(fh + 1) * FH]
                            .rearrange("(a p) f -> p a f", p=128))
                    for mi in range(3):
                        n_in = DC if mi == 2 else FS
                        pieces = 4 if n_in >= 4 else 2
                        for pc in range(pieces):
                            plist.append((mi, pc))
                    npc = len(plist)

                    def emit_dma(qi):
                        mi, pc = plist[qi]
                        n_in = DC if mi == 2 else FS
                        wid = FH if mi == 2 else D
                        pieces = 4 if n_in >= 4 else 2
                        pb = n_in // pieces
                        nb = nat_ctr[0] % 6
                        nat_ctr[0] += 1
                        nat_tok[(k, mi, pc)] = (nb, E_.dma(
                            "gpsimd",
                            nat[nb].ap()[:, 0:pb * wid].rearrange(
                                "p (a d) -> p a d", a=pb),
                            srcs[mi][:, pc * pb:(pc + 1) * pb, :],
                            s_nat[nb], waits=[(s_tm, nat_tr[nb])]))

                    for qi in range(min(6, npc)):
                        emit_dma(qi)
                    yield
                    ogrp = [0]
                    aux0 = prev_tr_evac
                    if prev_g_evac is not None:
                        aux0 = max(aux0 or 0, prev_g_evac)
                    tr_ev2 = [prev_tr_evac, aux0]
                    for qi, (mi, pc) in enumerate(plist):
                        is_w3 = (mi == 2)
                        n_out = FS if is_w3 else DC
                        n_in = DC if is_w3 else FS
                        pieces = 4 if n_in >= 4 else 2
                        pb = n_in // pieces
                        nb, tn = nat_tok[(k, mi, pc)]
                        natb = nat[nb]
                        wts = [(s_nat[nb], tn)]
                        if k >= 2:
                            wts += [(s_tm, sub_done[k - 2]),
                                    (s_scw[p], scw_tok[k - 2])]
                        for o in range(n_out):
                            bank = (ps_tr, ps_aux)[ogrp[0] % 2]
                            for bl in range(pb):
                                scol = bl * (FH if is_w3 else D) + o * 128
                                E_.mm(bank[:, bl * 128:(bl + 1) * 128],
                                      natb[:, scol:scol + 128],
                                      self.ident[:], start=True, stop=True,
                                      inc=(s_tm if bl == pb - 1 else None),
                                      waits=wts +
                                            [(s_ve, tr_ev2[ogrp[0] % 2])])
                            prev_tr_mm = s_tm.v
                            nat_tr[nb] = s_tm.v
                            if is_w3:
                                dcol = o * D + pc * pb * 128
                            else:
                                dcol = o * FH + pc * pb * 128
                            tr_ev2[ogrp[0] % 2] = E_.vop(
                                "tensor_copy",
                                wdsts[mi][p][:, dcol:dcol + pb * 128],
                                bank[:, 0:pb * 128],
                                inc=s_ve, waits=[(s_tm, prev_tr_mm)])
                            prev_tr_evac = tr_ev2[ogrp[0] % 2]
                            ogrp[0] += 1
                        if qi + 6 < npc:
                            emit_dma(qi + 6)
                        if pc == pieces - 1:
                            scw_tok[k] = E_.dma(
                                "sync", wscrs[mi][e, fh], wdsts[mi][p][:],
                                s_scw[p], waits=[(s_ve, prev_tr_evac)])
                        yield
                    trv_tok[k] = prev_tr_evac
                    wready[k] = prev_tr_evac
                    wsem[k] = s_ve

                def compute(k, feed=None):
                    nonlocal prev_sil, prev_htt, prev_h1mm, prev_h2mm
                    e, p = k // 2, k % 2
                    if feed is not None:
                        next(feed, None)
                    for g in range(NG):
                        hb = hid[g % 2]
                        for fs in range(FS):
                            w_h = [(wsem[k], wready[k])]
                            for dc in range(DC):
                                E_.mm(ps_h1[:],
                                      w1b[p][:, dc * FH + fs * 128:
                                             dc * FH + (fs + 1) * 128],
                                      xnT[:, dc * T + g * 512:
                                          dc * T + (g + 1) * 512],
                                      start=(dc == 0), stop=(dc == DC - 1),
                                      inc=s_tm if dc == DC - 1 else None,
                                      waits=w_h + [(s_se, prev_sil),
                                                   (s_ve, lnt[g * 4 + 3])])
                            t_h1 = s_tm.v
                            for dc in range(DC):
                                E_.mm(ps_h2[:],
                                      w2b[p][:, dc * FH + fs * 128:
                                             dc * FH + (fs + 1) * 128],
                                      xnT[:, dc * T + g * 512:
                                          dc * T + (g + 1) * 512],
                                      start=(dc == 0), stop=(dc == DC - 1),
                                      inc=s_tm if dc == DC - 1 else None,
                                      waits=[(s_ve, prev_htt)])
                            t_h2 = s_tm.v
                            prev_sil = E_.act(
                                asb[0][:], ps_h1[:], AF.Silu, inc=s_se,
                                waits=[(s_tm, t_h1),
                                       (s_ve, prev_htt)])
                            prev_htt = E_.vop(
                                "tensor_tensor",
                                hb[:, fs * 512:(fs + 1) * 512],
                                asb[0][:], ps_h2[:], op=ALU.mult,
                                inc=s_ve,
                                waits=[(s_se, prev_sil), (s_tm, t_h2),
                                       (s_tm, hid_user[g % 2])])
                        htt_all = prev_htt
                        for tc in range(4):
                            tile = g * 4 + tc
                            pso = ps_oa if tc % 2 == 0 else ps_ob
                            wts = [(s_ve, htt_all),
                                   (s_se, oev_last[tc % 2])]
                            for fs in range(FS):
                                for dh in range(2):
                                    E_.mm(pso[:, dh * 512:(dh + 1) * 512],
                                          hb[:, fs * 512 + tc * 128:
                                             fs * 512 + (tc + 1) * 128],
                                          w3b[p][:, fs * D + dh * 512:
                                                 fs * D + (dh + 1) * 512],
                                          start=(fs == 0), stop=(fs == FS - 1),
                                          inc=(s_tm if fs == FS - 1 and
                                               dh == 1 else None),
                                          waits=wts)
                            t_out = s_tm.v
                            hid_user[g % 2] = t_out
                            slot = tile % 2
                            ev = E_.act(
                                tmp[slot][:], pso[:], AF.Copy, bias=0.0,
                                scale=gates[:, e * NTT + tile:
                                            e * NTT + tile + 1],
                                inc=s_se,
                                waits=[(s_tm, t_out), (s_ve, t_gt),
                                       (s_rmw[(tile - 2) % NTT],
                                        rmw_t[(tile - 2) % NTT])])
                            oev_last[tc % 2] = ev
                            rmw_t[tile] = E_.dma(
                                "gpsimd", dst[tile * 128:(tile + 1) * 128, :],
                                tmp[slot][:], s_rmw[tile], accum=True,
                                waits=[(s_se, ev), (s_init, t_init),
                                       (s_rmw[tile], rmw_t[tile])])
                        if feed is not None:
                            for _ in range(4):
                                next(feed, None)
                    sub_done[k] = s_tm.v

                if build_scratch:
                    for _ in weights_gen(0):
                        pass
                    for k in range(NSUB):
                        feed = weights_gen(k + 1) if k + 1 < NSUB else None
                        compute(k, feed=feed)
                        if feed is not None:
                            for _ in feed:
                                pass
                else:
                    weights_dma(0)
                    weights_dma(1)
                    for k in range(NSUB):
                        compute(k)
                        if k + 2 < NSUB:
                            weights_dma(k + 2)

                for i in range(NTT):
                    E_.wait_on("gpsimd", [(s_rmw[i], rmw_t[i])])
                E_.wait_on("gpsimd",
                           [(s, s.v) for s in s_nat] + [(s_wg, s_wg.v)])
                E_.wait_on("sync", [(s_scw[0], s_scw[0].v),
                                    (s_scw[1], s_scw[1].v),
                                    (s_wld[0], s_wld[0].v),
                                    (s_wld[1], s_wld[1].v),
                                    (s_init, t_init)])
                for p in range(2):
                    E_.wait_on("sync", [(s_x[p], s_x[p].v)])

            E_.run(tag + "blk", gen)

    def attn_phase(self, E_, src, dst):
        nc = self.nc
        st1 = contextlib.ExitStack()
        with st1:
            sb = lambda n, sh, dt: self.sb("a_" + n, sh, dt, st1)
            xnT = sb("xnT", [128, DC * T], BF16)
            y_st = sb("yst", [128, D], F32)
            xst = [sb(f"xst{i}", [128, D], F32) for i in range(2)]
            y2st = [sb(f"y2st{i}", [128, D], F32) for i in range(2)]
            stats = [sb(f"stats{i}", [128, 12], F32) for i in range(2)]
            mv = [sb(f"mv{i}", [128, 2], F32) for i in range(2)]
            veps = [sb(f"veps{i}", [128, 1], F32) for i in range(2)]
            lnv = [sb(f"lnv{i}", [128, 1], F32) for i in range(2)]
            rstd = [sb(f"rstd{i}", [128, 2], F32) for i in range(2)]
            gbc = sb("gbc", [128, D], F32)
            bbc = sb("bbc", [128, D], F32)
            obc = sb("obc", [128, D], F32)
            inb = sb("inb", [128, 24], F32)
            vbc = sb("vbc", [128, D], F32)
            vbc4 = sb("vbc4", [128, 512], F32)
            ls = [sb(f"ls{i}", [128, DIN], F32) for i in range(2)]
            negb = sb("negb", [128, NH], F32)
            negbp = sb("negbp", [128, NP], F32)
            rr_s = sb("rr_s", [2, 512], F32)
            nrm = sb("nrm", [2, 16], F32)
            bnd = sb("bnd", [2, 8], F32)
            bndb = sb("bndb", [2, 1], BF16)
            ones2 = sb("ones2", [128, 2], BF16)
            selb = sb("selb", [2, 256], BF16)
            onz = sb("onz", [128, 2], BF16)
            zno = sb("zno", [128, 2], BF16)
            onescol = sb("onescol", [128, 1], BF16)
            self_f = sb("self", [2, 128], F32)
            rr2 = sb("rr2", [2, 512], F32)
            rbc = sb("rbc", [128, 512], F32)

            s_x = [self.sem("xld0"), self.sem("xld1")]
            s_pj = [self.sem("pj0"), self.sem("pj1")]
            s_xe = [self.sem("xe0"), self.sem("xe1")]
            s_lc = [self.sem("lc0"), self.sem("lc1")]
            s_lr = [self.sem("lr0"), self.sem("lr1")]
            s_tw = self.sem("ttw")
            s_tm = self.sem("tm")
            s_ve = self.sem("ve")
            s_se = self.sem("se")

            stP = contextlib.ExitStack()
            inbf = self.sb("p_inbf", [1, 3 * D], F32, stP)
            lnwf = self.sb("p_lnwf", [1, D], F32, stP)
            lnbf = self.sb("p_lnbf", [1, D], F32, stP)
            opbf = self.sb("p_opbf", [1, D], F32, stP)
            stP.close()
            st2 = contextlib.ExitStack()
            sb2 = lambda n, sh, dt: self.sb("t_" + n, sh, dt, st2)
            xtok = sb2("xtok", [128, NTT * D], BF16)
            w0T = sb2("w0T", [128, LH * 2 * DIN], BF16)
            w1a = sb2("w1a", [128, LH * 2 * DIN], BF16)
            w2T = sb2("w2T", [128, LH * 2 * DIN], BF16)
            twn = sb2("twn", [128, 2 * DIN], BF16)
            w0nT = sb2("w0nT", [128, 2 * DIN], BF16)
            w2nT = sb2("w2nT", [128, 2 * DIN], BF16)
            w1n = sb2("w1n", [128, 2 * DIN], BF16)
            w1nT = sb2("w1nT", [128, 2 * DIN], BF16)
            gt_s = sb2("gt", [128, 2048], F32)
            sg_s = sb2("sg", [128, 2048], F32)
            g_s = sb2("g", [128, 2048], F32)
            dh_s = sb2("dh", [128, 2048], F32)
            h_s = sb2("h", [128, 2048], F32)
            hq_s = sb2("hq", [128, 2048], F32)
            hid_b = sb2("hid", [128, 2048], BF16)
            dpre_b = sb2("dpre", [128, 2048], BF16)
            dg_b = sb2("dg", [128, 2048], BF16)
            hg_b = sb2("hg", [128, 2048], BF16)
            hgT_b = sb2("hgT", [128, 2048], BF16)
            pst2 = contextlib.ExitStack()
            ps_tr = self.psum("a_tr", [128, 512], pst2)
            pA = self.psum("a_pA", [128, 512], pst2)
            pB = self.psum("a_pB", [128, 512], pst2)
            pC = self.psum("a_pC", [128, 512], pst2)
            st2.close()
            pst2.close()
            st3 = contextlib.ExitStack()
            sb3 = lambda n, sh, dt: self.sb("q_" + n, sh, dt, st3)
            Qp = [sb3(f"Qp{i}", [128, T], BF16) for i in range(2)]
            Kp = [sb3(f"Kp{i}", [128, T], BF16) for i in range(2)]
            Vp = [sb3(f"Vp{i}", [128, NTT * 128], BF16) for i in range(2)]
            pjn = sb3("pjn", [128, 3 * D], BF16)
            pjT = [sb3(f"pjT{i}", [128, DC * 384], BF16) for i in range(2)]
            PT = [sb3(f"PT{i}", [128, 1024], BF16) for i in range(2)]
            sqb = sb3("sqb", [128, T], BF16)
            aoT = sb3("aoT", [128, DC * T], BF16)
            opT = sb3("opT", [128, DC * D], BF16)
            x1t = [sb3(f"x1t{i}", [128, D], F32) for i in range(2)]
            lct = [sb3(f"lct{i}", [128, D], F32) for i in range(2)]
            x2t = [sb3(f"x2t{i}", [128, D], F32) for i in range(2)]
            pst3 = contextlib.ExitStack()
            stAB = [self.psum(f"a_st{i}", [128, 1024], pst3) for i in range(2)]
            pv = [self.psum(f"a_pv{i}", [128, 512], pst3) for i in range(2)]
            dn = self.psum("a_dn", [128, 512], pst3)
            qkb = self.psum("a_qkb", [128, 512], pst3)
            pst3.close()

            def gen():
                E_.vop("memset", ones2[0:64, 0:1], 1.0)
                E_.vop("memset", ones2[64:128, 0:1], 0.0)
                E_.vop("memset", ones2[0:64, 1:2], 0.0)
                E_.vop("memset", ones2[64:128, 1:2], 1.0)
                E_.vop("memset", onescol[:], 1.0)
                E_.vop("memset", onz[:, 0:1], 1.0)
                E_.vop("memset", onz[:, 1:2], 0.0)
                E_.vop("memset", zno[:, 0:1], 0.0)
                t_ms = E_.vop("memset", zno[:, 1:2], 1.0, inc=s_ve)
                E_.dma("gpsimd", selb[:], self.selbD, s_tw)
                E_.dma("gpsimd", self_f[:], self.selfD, s_tw)

                E_.dma("gpsimd", lnwf[:], self.ln_w.rearrange("(a d) -> a d", a=1), s_tw)
                E_.dma("gpsimd", lnbf[:], self.ln_b.rearrange("(a d) -> a d", a=1), s_tw)
                E_.dma("gpsimd", opbf[:], self.opb.rearrange("(a d) -> a d", a=1), s_tw)
                E_.dma("gpsimd", inbf[:], self.ipb.rearrange("(a d) -> a d", a=1), s_tw)

                t_small = s_tw.v

                prev_ev = t_ms
                for (flat, off, dstb) in ((lnwf, 0, gbc), (lnbf, 0, bbc),
                                          (opbf, 0, obc),
                                          (inbf, 2 * D, vbc)):
                    for cg in range(2):
                        E_.mm(ps_tr[:], self.ones_f[0:1, :],
                              flat[0:1, off + cg * 512:off + (cg + 1) * 512],
                              start=True, stop=True, inc=s_tm,
                              waits=[(s_tw, t_small), (s_ve, prev_ev)])
                        prev_ev = E_.vop("tensor_copy",
                                         dstb[:, cg * 512:(cg + 1) * 512],
                                         ps_tr[:], inc=s_ve,
                                         waits=[(s_tm, s_tm.v)])
                for c in range(24):
                    E_.mm(ps_tr[:, 0:1],
                          inbf[0:1, c * 128:(c + 1) * 128],
                          self.ones_f[0:1, 0:1], start=True, stop=True,
                          inc=s_tm,
                          waits=[(s_tw, t_small), (s_ve, prev_ev)])
                    prev_ev = E_.vop("tensor_copy", inb[:, c:c + 1],
                                     ps_tr[:, 0:1], inc=s_ve,
                                     waits=[(s_tm, s_tm.v)])
                t_bcast = prev_ev
                t_flatmm = s_tm.v
                for h in range(LH):
                    E_.dma("gpsimd", w1a.ap().rearrange(
                               "p (h a e) -> p h a e", h=LH, a=2)[:, h],
                           self.tw1[h].rearrange("(a p) e -> p a e", p=128),
                           s_tw, waits=[(s_tm, t_flatmm)])
                t_small = s_tw.v

                apl = [None] * NTT
                lnt = [None] * NTT
                prev_tr_evac = t_bcast
                lnt_mm = [None]

                def ln_stage1(tt):
                    p = tt % 2
                    xld = E_.dma("sync", xst[p][:],
                                 src[tt * 128:(tt + 1) * 128, :], s_x[p],
                                 serial=True,
                                 waits=[(s_ve, apl[tt - 2])] if tt >= 2 else [])
                    E_.vop("bn_stats", stats[p][:, 0:6], xst[p][:, 0:512],
                           waits=[(s_x[p], xld)])
                    tbn = E_.vop("bn_stats", stats[p][:, 6:12],
                                 xst[p][:, 512:1024], inc=s_ve)
                    tag_ = E_.vop("bn_aggr", mv[p][:], stats[p][:, 0:12],
                                  inc=s_ve, waits=[(s_ve, tbn)])
                    tve = E_.vop("tensor_single_scalar", veps[p][:],
                                 mv[p][:, 1:2], EPS, ALU.add, inc=s_ve,
                                 waits=[(s_ve, tag_)])
                    tln = E_.act(lnv[p][:], veps[p][:], AF.Ln, inc=s_se,
                                 waits=[(s_ve, tve)])
                    tsq = E_.act(rstd[p][:, 0:1], lnv[p][:], AF.Exp,
                                 scale=-0.5, inc=s_se, waits=[(s_se, tln)])
                    t_ap1 = E_.vop(
                        "tensor_scalar", y2st[p][:], xst[p][:],
                        mv[p][:, 0:1], rstd[p][:, 0:1],
                        op0=ALU.subtract, op1=ALU.mult, inc=s_ve,
                        waits=[(s_se, tsq), (s_ve, tag_)])
                    E_.vop("bn_stats", stats[p][:, 0:6], y2st[p][:, 0:512],
                           waits=[(s_ve, t_ap1)])
                    tbn = E_.vop("bn_stats", stats[p][:, 6:12],
                                 y2st[p][:, 512:1024], inc=s_ve)
                    tag2 = E_.vop("bn_aggr", mv[p][:], stats[p][:, 0:12],
                                  inc=s_ve, waits=[(s_ve, tbn)])
                    tve2 = E_.vop("tensor_single_scalar", veps[p][:],
                                  mv[p][:, 1:2], EPS, ALU.add, inc=s_ve,
                                  waits=[(s_ve, tag2)])
                    tln2 = E_.act(lnv[p][:], veps[p][:], AF.Ln, inc=s_se,
                                  waits=[(s_ve, tve2)])
                    tsq2 = E_.act(rstd[p][:, 1:2], lnv[p][:], AF.Exp,
                                  scale=-0.5, inc=s_se, waits=[(s_se, tln2)])
                    return (tag2, tsq2)

                def ln_stage2(tt, toks):
                    nonlocal prev_tr_evac
                    p = tt % 2
                    tag2, tsq2 = toks
                    tz = E_.vop("tensor_scalar", y2st[p][:], y2st[p][:],
                                mv[p][:, 0:1], rstd[p][:, 1:2],
                                op0=ALU.subtract, op1=ALU.mult, inc=s_ve,
                                waits=[(s_se, tsq2), (s_ve, tag2)])
                    tz = E_.vop("tensor_tensor", y2st[p][:], y2st[p][:],
                                gbc[:], op=ALU.mult, inc=s_ve,
                                waits=[(s_ve, tz)])
                    apl[tt] = E_.vop(
                        "tensor_tensor",
                        xtok[:, tt * D:(tt + 1) * D], y2st[p][:],
                        bbc[:], op=ALU.add, inc=s_ve, waits=[(s_ve, tz)])
                    for dcg in range(2):
                        for dcl in range(4):
                            dc = dcg * 4 + dcl
                            E_.mm(ps_tr[:, dcl * 128:(dcl + 1) * 128],
                                  xtok[:, tt * D + dc * 128:
                                       tt * D + (dc + 1) * 128],
                                  self.ident[:], start=True, stop=True,
                                  inc=s_tm if dcl == 3 else None,
                                  waits=[(s_ve, apl[tt]),
                                         (s_ve, prev_tr_evac)])
                        tmm = s_tm.v
                        xv = xnT.ap().rearrange("p (dc t) -> p dc t", dc=DC)
                        prev_tr_evac = E_.vop(
                            "tensor_copy",
                            xv[:, dcg * 4:(dcg + 1) * 4,
                               tt * 128:(tt + 1) * 128],
                            ps_tr.ap().rearrange("p (a b) -> p a b", a=4),
                            inc=s_ve, waits=[(s_tm, tmm)])
                    lnt_mm[0] = s_tm.v
                    lnt[tt] = prev_tr_evac

                ln_pend = {}

                def ln_tile(tt):
                    ln_pend[tt] = ln_stage1(tt)
                    if tt - 1 in ln_pend:
                        ln_stage2(tt - 1, ln_pend.pop(tt - 1))

                for tt in range(CCT):
                    ln_tile(tt)
                if CCT - 1 in ln_pend:
                    ln_stage2(CCT - 1, ln_pend.pop(CCT - 1))

                w1v = w1a.ap().rearrange("p (h a e) -> p h a e", h=LH, a=2)
                w0v = w0T.ap().rearrange("p (h a e) -> p h a e", h=LH, a=2)
                w2v = w2T.ap().rearrange("p (h a e) -> p h a e", h=LH, a=2)
                tw_tr = None
                for h in range(LH):
                    for (wsrc, wdstv) in ((self.tw0, w0v), (self.tw2, w2v)):
                        tn = E_.dma("gpsimd",
                                    twn.ap().rearrange("p (a e) -> p a e", a=2),
                                    wsrc[h].rearrange("(a p) d -> p a d",
                                                      p=128),
                                    s_tw, waits=[(s_tm, tw_tr)])
                        tloc = s_tw.v
                        for dc in range(2):
                            for ec in range(2):
                                E_.mm(ps_tr[:, (dc * 2 + ec) * 128:
                                            (dc * 2 + ec + 1) * 128],
                                      twn[:, ec * DIN + dc * 128:
                                          ec * DIN + (dc + 1) * 128],
                                      self.ident[:], start=True, stop=True,
                                      inc=(s_tm if dc == 1 and ec == 1
                                           else None),
                                      waits=[(s_tw, tloc),
                                             (s_ve, prev_tr_evac)])
                        tw_tr = s_tm.v
                        prev_tr_evac = E_.vop(
                            "tensor_copy", wdstv[:, h],
                            ps_tr.ap().rearrange("p (a e) -> p a e", a=2),
                            inc=s_ve, waits=[(s_tm, tw_tr)])
                t_twdone = prev_tr_evac

                lct_w = [[None] * 8 for _ in range(2)]
                lsu = [None, None]
                pfree = [t_twdone, t_twdone, t_twdone]
                tpre = t_twdone
                for n in range(NCK):
                    for h in range(LH):
                        if n == 0 and NCK > 1:
                            for tt2 in range(CCT + 2 * h,
                                             min(NTT, CCT + 2 * h + 2)):
                                ln_tile(tt2)
                            if h == LH - 1:
                                for tt2 in list(ln_pend):
                                    ln_stage2(tt2, ln_pend.pop(tt2))

                        fams = ((w0v[:, h], gt_s, None, "gate", None),
                                (w2v[:, h], g_s, None, None, None),
                                (w1v[:, h], dh_s, -1.0, None, None))
                        for fi, (wap, outs, scl, sfn, souts) in enumerate(fams):
                            for ct in range(CCT):
                                pX = (pA, pB)[(fi * CCT + ct) % 2]
                                for dc in range(2):
                                    E_.mm(pX[:, 0:DIN],
                                          xnT[:, (h * 2 + dc) * T + n * CHK
                                              + ct * 128:
                                              (h * 2 + dc) * T + n * CHK
                                              + (ct + 1) * 128],
                                          wap[:, dc],
                                          start=(dc == 0), stop=(dc == 1),
                                          inc=s_tm if dc == 1 else None,
                                          waits=[(s_ve, lnt[n * CCT
                                                            + CCT - 1]),
                                                 (s_ve, pfree[(fi * CCT + ct)
                                                              % 2])])
                                tmm = s_tm.v
                                cs = slice(ct * DIN, (ct + 1) * DIN)
                                if sfn == "gate":
                                    E_.act(h_s[:, cs], pX[:, 0:DIN], AF.Silu,
                                           inc=s_se, waits=[(s_tm, tmm)])
                                    E_.act(sg_s[:, cs], pX[:, 0:DIN],
                                           AF.Sigmoid, inc=s_se)
                                ev = E_.vop(
                                    "tensor_copy", outs[:, cs], pX[:, 0:DIN],
                                    inc=s_ve,
                                    waits=[(s_tm, tmm)] +
                                          ([(s_se, s_se.v)]
                                           if sfn == "gate" else []))
                                if scl is not None:
                                    ev = E_.vop("tensor_single_scalar",
                                                outs[:, cs], outs[:, cs],
                                                -1.0, ALU.mult, inc=s_ve,
                                                waits=[(s_ve, ev)])
                                pfree[(fi * CCT + ct) % 2] = ev
                        W_ = slice(0, CCT * DIN)
                        tb0 = s_ve.v
                        c2 = E_.vop("tensor_tensor", hid_b[:, W_], h_s[:, W_],
                                    g_s[:, W_], op=ALU.mult, inc=s_ve,
                                    waits=[(s_ve, tb0), (s_se, s_se.v)])
                        c3 = E_.vop("tensor_tensor", dg_b[:, W_], dh_s[:, W_],
                                    h_s[:, W_], op=ALU.mult, inc=s_ve,
                                    waits=[(s_ve, c2)])
                        c4 = E_.vop("tensor_tensor", g_s[:, W_], dh_s[:, W_],
                                    g_s[:, W_], op=ALU.mult, inc=s_ve,
                                    waits=[(s_ve, c3)])
                        c5 = E_.vop("tensor_scalar", hq_s[:, W_], sg_s[:, W_],
                                    -1.0, 1.0, op0=ALU.mult, op1=ALU.add,
                                    inc=s_ve, waits=[(s_ve, tb0)])
                        c6 = E_.vop("tensor_tensor", hq_s[:, W_], h_s[:, W_],
                                    hq_s[:, W_], op=ALU.mult, inc=s_ve,
                                    waits=[(s_ve, c5)])
                        c7 = E_.vop("tensor_tensor", hq_s[:, W_], sg_s[:, W_],
                                    hq_s[:, W_], op=ALU.add, inc=s_ve,
                                    waits=[(s_ve, c6)])
                        c9 = E_.vop("tensor_tensor", dpre_b[:, W_], g_s[:, W_],
                                    hq_s[:, W_], op=ALU.mult, inc=s_ve,
                                    waits=[(s_ve, c4), (s_ve, c7)])
                        upds = ((hid_b, w1v, w1n, ALU.add, c2),
                                (dpre_b, w0v, w0nT, ALU.subtract, c9),
                                (dg_b, w2v, w2nT, ALU.subtract, c3))
                        for ui, (movb, wbase, outw, op, crdy) in \
                                enumerate(upds):
                            for dc2 in range(2):
                                pX = (pA, pB)[(ui * 2 + dc2) % 2]
                                for ct in range(CCT):
                                    E_.mm(pX[:, 0:DIN],
                                          xtok[:, (n * CCT + ct) * D
                                               + h * DIN + dc2 * 128:
                                               (n * CCT + ct) * D + h * DIN
                                               + (dc2 + 1) * 128],
                                          movb[:, ct * DIN:(ct + 1) * DIN],
                                          start=(ct == 0),
                                          stop=(ct == CCT - 1),
                                          inc=s_tm if ct == CCT - 1 else None,
                                          waits=[(s_ve, crdy),
                                                 (s_ve, pfree[(ui * 2 + dc2)
                                                              % 2])])
                                tmm = s_tm.v
                                ev = E_.vop(
                                    "tensor_tensor",
                                    outw[:, dc2 * DIN:(dc2 + 1) * DIN],
                                    wbase[:, h, dc2], pX[:, 0:DIN], op=op,
                                    inc=s_ve, waits=[(s_tm, tmm)])
                                pfree[(ui * 2 + dc2) % 2] = ev
                        tw1n = pfree[(2 * 2 + 1) % 2]
                        for ec in range(2):
                            for dc2 in range(2):
                                E_.mm(ps_tr[:, (ec * 2 + dc2) * 128:
                                            (ec * 2 + dc2 + 1) * 128],
                                      w1n[:, dc2 * DIN + ec * 128:
                                          dc2 * DIN + (ec + 1) * 128],
                                      self.ident[:], start=True, stop=True,
                                      inc=(s_tm if ec == 1 and dc2 == 1
                                           else None),
                                      waits=[(s_ve, pfree[0]),
                                             (s_ve, pfree[1]),
                                             (s_ve, tpre)])
                        tmm = s_tm.v
                        tpre = E_.vop(
                            "tensor_copy",
                            w1nT.ap().rearrange("p (a e) -> p a e", a=2),
                            ps_tr.ap().rearrange("p (a e) -> p a e", a=2),
                            inc=s_ve, waits=[(s_tm, tmm)])
                        banksD = [pA, pB, pC, ps_tr]
                        bfree = [(s_ve, pfree[0]), (s_ve, pfree[1]),
                                 (s_ve, pfree[2]), (s_ve, tpre)]
                        for ct in range(CCT):
                            cs = slice(ct * DIN, (ct + 1) * DIN)
                            bq = banksD[(2 * ct) % 4]
                            bq2 = banksD[(2 * ct + 1) % 4]
                            w0_ = bfree[(2 * ct) % 4]
                            w2_ = bfree[(2 * ct + 1) % 4]
                            for dc2 in range(2):
                                E_.mm(bq[:, 0:DIN],
                                      xnT[:, (h * 2 + dc2) * T + n * CHK
                                          + ct * 128:
                                          (h * 2 + dc2) * T + n * CHK
                                          + (ct + 1) * 128],
                                      w0nT[:, dc2 * DIN:(dc2 + 1) * DIN],
                                      start=(dc2 == 0), stop=(dc2 == 1),
                                      inc=s_tm if dc2 == 1 else None,
                                      waits=[w0_] if w0_ else [])
                            tgq = s_tm.v
                            for dc2 in range(2):
                                E_.mm(bq2[:, 0:DIN],
                                      xnT[:, (h * 2 + dc2) * T + n * CHK
                                          + ct * 128:
                                          (h * 2 + dc2) * T + n * CHK
                                          + (ct + 1) * 128],
                                      w2nT[:, dc2 * DIN:(dc2 + 1) * DIN],
                                      start=(dc2 == 0), stop=(dc2 == 1),
                                      inc=s_tm if dc2 == 1 else None,
                                      waits=[w2_] if w2_ else [])
                            tg2 = s_tm.v
                            tsg2 = E_.act(h_s[:, cs], bq[:, 0:DIN],
                                          AF.Silu, inc=s_se,
                                          waits=[(s_tm, tgq)])
                            bfree[(2 * ct) % 4] = (s_se, tsg2)
                            thg = E_.vop("tensor_tensor", hg_b[:, cs],
                                         h_s[:, cs], bq2[:, 0:DIN],
                                         op=ALU.mult, inc=s_ve,
                                         waits=[(s_se, tsg2), (s_tm, tg2)])
                            bfree[(2 * ct + 1) % 4] = (s_ve, thg)
                        thg_all = s_ve.v
                        trw = [w for w in bfree if w]
                        for ec in range(2):
                            for ctg in range(0, CCT, 4):
                                gw = min(4, CCT - ctg)
                                for ctl in range(gw):
                                    ct = ctg + ctl
                                    E_.mm(ps_tr[:, ctl * 128:(ctl + 1) * 128],
                                          hg_b[:, ct * DIN + ec * 128:
                                               ct * DIN + (ec + 1) * 128],
                                          self.ident[:], start=True,
                                          stop=True,
                                          inc=s_tm if ctl == gw - 1 else None,
                                          waits=[(s_ve, thg_all),
                                                 (s_ve, tpre)] + trw)
                                tmm = s_tm.v
                                tpre = E_.vop(
                                    "tensor_copy",
                                    hgT_b[:, ec * CHK + ctg * 128:
                                          ec * CHK + (ctg + gw) * 128],
                                    ps_tr[:, 0:gw * 128],
                                    inc=s_ve, waits=[(s_tm, tmm)])
                        for ct in range(CCT):
                            for ec in range(2):
                                E_.mm(pC[:, 0:DIN],
                                      hgT_b[:, ec * CHK + ct * 128:
                                            ec * CHK + (ct + 1) * 128],
                                      w1nT[:, ec * DIN:(ec + 1) * DIN],
                                      start=(ec == 0), stop=(ec == 1),
                                      inc=s_tm if ec == 1 else None,
                                      waits=[(s_ve, tpre),
                                             (s_ve, pfree[2]),
                                             (s_lc[ct % 2], lsu[ct % 2])]
                                            + trw)
                            tmm = s_tm.v
                            ev = E_.vop("tensor_copy", ls[ct % 2][:],
                                        pC[:, 0:DIN], inc=s_ve,
                                        waits=[(s_tm, tmm),
                                               (s_lc[ct % 2], lsu[ct % 2])])
                            pfree[2] = ev
                            lsu[ct % 2] = E_.dma(
                                "gpsimd",
                                self.lacts[n * CHK + ct * 128:
                                           n * CHK + (ct + 1) * 128,
                                           h * DIN:(h + 1) * DIN],
                                ls[ct % 2][:], s_lc[ct % 2],
                                waits=[(s_ve, ev)])
                ttt_tm = s_tm.v
                ttt_ve = s_ve.v
                ttt_se = s_se.v

                pj_tok = [None] * NP
                qk_done = [None] * NP
                pv_done = [None] * NP
                negb_tok = [None] * NP
                qkv_tok = [None] * NP
                ptu = [None, None]
                qb_free = ttt_ve
                pjn_free = None
                stu = [ttt_se, ttt_se]
                pv_free = [ttt_ve, ttt_ve]
                dn_free = [None, None]
                dn_par = [0]

                def proj_qkv_norms(p):
                    nonlocal qb_free, pjn_free
                    pp = p % 2
                    wts = [(s_tm, ttt_tm), (s_ve, ttt_ve), (s_se, ttt_se),
                           (s_tm, pjn_free)]
                    if p >= 2:
                        wts += [(s_tm, qk_done[p - 2]),
                                (s_tm, pv_done[p - 2])]
                    tpj = None
                    for m in range(3):
                        tpj = E_.dma(
                            "gpsimd", pjn[:, m * D:(m + 1) * D],
                            self.ipw[m * D + p * 128:m * D + (p + 1) * 128,
                                     :],
                            s_pj[pp], waits=wts)
                    pj_tok[p] = tpj
                    for dc in range(DC):
                        for m in range(3):
                            E_.mm(qkb[:, m * 128:(m + 1) * 128],
                                  pjn[:, m * D + dc * 128:
                                      m * D + (dc + 1) * 128],
                                  self.ident[:], start=True, stop=True,
                                  inc=s_tm if m == 2 else None,
                                  waits=[(s_pj[pp], tpj), (s_ve, qb_free),
                                         (s_tm, ttt_tm)])
                        tmm = s_tm.v
                        qb_free = E_.vop(
                            "tensor_copy", pjT[pp][:, dc * 384:dc * 384 + 384],
                            qkb[:, 0:384], inc=s_ve, waits=[(s_tm, tmm)])
                    pjn_free = s_tm.v
                    ev = None
                    for (mi, dstq, bias_c, scl) in ((0, Qp[pp], p, 0.125),
                                                    (1, Kp[pp], 8 + p, None)):
                        for tg in range(NG):
                            for dc in range(DC):
                                E_.mm(qkb[:],
                                      pjT[pp][:, dc * 384 + mi * 128:
                                              dc * 384 + (mi + 1) * 128],
                                      xnT[:, dc * T + tg * 512:
                                          dc * T + (tg + 1) * 512],
                                      start=(dc == 0), stop=(dc == DC - 1),
                                      inc=s_tm if dc == DC - 1 else None,
                                      waits=[(s_ve, qb_free),
                                             (s_se, ev),
                                             (s_tw, t_small)])
                            tmm = s_tm.v
                            ev = E_.act(dstq[:, tg * 512:(tg + 1) * 512],
                                        qkb[:], AF.Identity,
                                        bias=inb[:, bias_c:bias_c + 1],
                                        scale=scl, inc=s_se,
                                        waits=[(s_tm, tmm)])
                    tv4 = None
                    for r in range(4):
                        tv4 = E_.vop("tensor_copy",
                                     vbc4[:, r * 128:(r + 1) * 128],
                                     vbc[:, p * 128:(p + 1) * 128],
                                     inc=s_ve, waits=[(s_ve, t_bcast)])
                    evv = tv4
                    for t4 in range(0, NTT, 4):
                        for tl in range(4):
                            tt = t4 + tl
                            for dc in range(DC):
                                E_.mm(qkb[:, tl * 128:(tl + 1) * 128],
                                      xnT[:, dc * T + tt * 128:
                                          dc * T + (tt + 1) * 128],
                                      pjT[pp][:, dc * 384 + 256:
                                              dc * 384 + 384],
                                      start=(dc == 0), stop=(dc == DC - 1),
                                      inc=(s_tm if dc == DC - 1 and tl == 3
                                           else None),
                                      waits=[(s_ve, evv), (s_se, ev),
                                             (s_tm, pv_done[p - 2] if p >= 2
                                              else None)])
                        tmm = s_tm.v
                        evv = E_.vop(
                            "tensor_tensor",
                            Vp[pp][:, t4 * 128:(t4 + 4) * 128],
                            qkb[:], vbc4[:], op=ALU.add, inc=s_ve,
                            waits=[(s_tm, tmm), (s_ve, t_bcast)])
                    qb_free = evv
                    tq = E_.vop("tensor_tensor", sqb[:], Qp[pp][:], Qp[pp][:],
                                op=ALU.mult, inc=s_ve, waits=[(s_se, ev)])
                    toks = []
                    for (qi, srcb) in ((0, Qp[pp]), (1, Kp[pp])):
                        if qi == 1:
                            tq = E_.vop("tensor_tensor", sqb[:], srcb[:],
                                        srcb[:], op=ALU.mult, inc=s_ve,
                                        waits=[(s_ve, toks[-1]),
                                               (s_se, ev)])
                        for cg in range(NG):
                            E_.mm(qkb[0:2, 0:512], ones2[:],
                                  sqb[:, cg * 512:(cg + 1) * 512],
                                  start=True, stop=True, inc=s_tm,
                                  waits=[(s_ve, tq), (s_ve, qb_free)])
                            qb_free = E_.vop(
                                "reduce_max",
                                nrm[0:2, qi * NG + cg:qi * NG + cg + 1],
                                qkb[0:2, 0:512], axis=AX.X, inc=s_ve,
                                waits=[(s_tm, s_tm.v)])
                            toks.append(qb_free)
                    b1 = E_.vop("reduce_max", bnd[0:2, 0:1], nrm[0:2, 0:NG],
                                axis=AX.X, inc=s_ve, waits=[(s_ve, toks[-1])])
                    b2 = E_.vop("reduce_max", bnd[0:2, 1:2],
                                nrm[0:2, NG:2 * NG],
                                axis=AX.X, inc=s_ve, waits=[(s_ve, b1)])
                    b3 = E_.vop("tensor_tensor", bnd[0:2, 2:3], bnd[0:2, 0:1],
                                bnd[0:2, 1:2], op=ALU.mult, inc=s_ve,
                                waits=[(s_ve, b2)])
                    b4 = E_.act(bnd[0:2, 3:4], bnd[0:2, 2:3], AF.Ln,
                                inc=s_se, waits=[(s_ve, b3)])
                    b5 = E_.act(bnd[0:2, 4:5], bnd[0:2, 3:4], AF.Exp,
                                scale=0.5, inc=s_se, waits=[(s_se, b4)])
                    b6a = E_.vop("tensor_single_scalar", bnd[0:2, 6:7],
                                 bnd[0:2, 4:5], 0.5, ALU.mult, inc=s_ve,
                                 waits=[(s_se, b5)])
                    b6b = E_.vop("tensor_single_scalar", bnd[0:2, 7:8],
                                 bnd[0:2, 4:5], -80.0, ALU.add, inc=s_ve,
                                 waits=[(s_ve, b6a)])
                    b6c = E_.vop("tensor_tensor", bnd[0:2, 6:7],
                                 bnd[0:2, 6:7], bnd[0:2, 7:8], op=ALU.max,
                                 inc=s_ve, waits=[(s_ve, b6b)])
                    b6 = E_.vop("tensor_single_scalar", bnd[0:2, 5:6],
                                bnd[0:2, 6:7], -1.0, ALU.mult, inc=s_ve,
                                waits=[(s_ve, b6c)])
                    b7 = E_.vop("tensor_copy", bndb[:], bnd[0:2, 5:6],
                                inc=s_ve, waits=[(s_ve, b6)])
                    for hl in range(2):
                        E_.mm(qkb[:, 0:1],
                              selb[:, hl * 128:(hl + 1) * 128],
                              bndb[:], start=True, stop=True, inc=s_tm,
                              waits=[(s_ve, b7), (s_ve, qb_free)])
                        qb_free = E_.vop(
                            "tensor_copy", negb[:, 2 * p + hl:2 * p + hl + 1],
                            qkb[:, 0:1], inc=s_ve, waits=[(s_tm, s_tm.v)])
                    qb_free = E_.vop("tensor_tensor", negbp[:, p:p + 1],
                                     negb[:, 2 * p:2 * p + 1],
                                     negb[:, 2 * p + 1:2 * p + 2],
                                     op=ALU.min, inc=s_ve,
                                     waits=[(s_ve, qb_free)])
                    negb_tok[p] = qb_free
                    qkv_tok[p] = qb_free

                def attn(p):
                    nonlocal qb_free
                    pp = p % 2
                    dn_free[1] = (s_ve, max(qb_free or 0,
                                            (dn_free[1] or (None, 0))[1]))

                    def emit_st(q0, kt):
                        kp = kt % 2
                        E_.mm(stAB[kp][:, 0:512],
                              Kp[pp][0:64, kt * 128:(kt + 1) * 128],
                              Qp[pp][0:64, q0:q0 + 512],
                              start=True, stop=True, inc=s_tm,
                              waits=[(s_se, stu[kp]),
                                     (s_ve, qkv_tok[p]),
                                     (s_ve, negb_tok[p])])
                        return E_.mm(stAB[kp][:, 512:1024],
                                     Kp[pp][64:128,
                                            kt * 128:(kt + 1) * 128],
                                     Qp[pp][64:128, q0:q0 + 512],
                                     start=True, stop=True, inc=s_tm)

                    last_st = None
                    groups = list(range(0, T, 512))
                    st_tok = {0: emit_st(groups[0], 0)}
                    for gi, q0 in enumerate(groups):
                        gp = gi % 2
                        pvb = pv[gp]
                        dnb = (dn, qkb)[dn_par[0] % 2]
                        dnw = dn_free[dn_par[0] % 2]
                        for kt in range(NTT):
                            kp = kt % 2
                            tb = st_tok[kt]
                            ea = E_.act(PT[kp][:], stAB[kp][:], AF.Exp,
                                        bias=negbp[:, p:p + 1], inc=s_se,
                                        waits=[(s_tm, tb),
                                               (s_tm, ptu[kp])])
                            stu[kp] = ea
                            if kt + 1 < NTT:
                                st_tok[kt + 1] = emit_st(q0, kt + 1)
                            elif gi + 1 < len(groups):
                                st_tok[0] = emit_st(groups[gi + 1], 0)
                            E_.mm(pvb[0:64, :],
                                  Vp[pp][:, kt * 128:kt * 128 + 64],
                                  PT[kp][:, 0:512],
                                  start=(kt == 0), stop=(kt == NTT - 1),
                                  waits=[(s_se, ea), (s_ve, pv_free[gp])])
                            E_.mm(pvb[64:128, :],
                                  Vp[pp][:, kt * 128 + 64:kt * 128 + 128],
                                  PT[kp][:, 512:1024],
                                  start=(kt == 0), stop=(kt == NTT - 1),
                                  skip_group_check=True,
                                  waits=[(s_se, ea)])
                            E_.mm(dnb[0:2, :], onz[:], PT[kp][:, 0:512],
                                  start=(kt == 0), stop=False,
                                  inc=None, skip_group_check=True,
                                  waits=[dnw] if dnw else [])
                            E_.mm(dnb[0:2, :], zno[:], PT[kp][:, 512:1024],
                                  start=False, stop=(kt == NTT - 1),
                                  inc=s_tm, skip_group_check=True)
                            ptu[kp] = s_tm.v
                            last_st = s_tm.v
                        r2 = E_.vop("reciprocal", rr2[0:2, :],
                                    dnb[0:2, :], inc=s_ve,
                                    waits=[(s_tm, last_st)])
                        E_.mm(dnb[0:128, :], self_f[:], rr2[:],
                              start=True, stop=True, inc=s_tm,
                              waits=[(s_ve, r2)], skip_group_check=True)
                        tbc = E_.vop("tensor_copy", rbc[:], dnb[0:128, :],
                                     inc=s_ve, waits=[(s_tm, s_tm.v)])
                        dn_free[dn_par[0] % 2] = (s_ve, tbc)
                        dn_par[0] += 1
                        pv_free[gp] = E_.vop(
                            "tensor_tensor",
                            aoT[:, p * T + q0:p * T + q0 + 512],
                            pvb[:], rbc[:], op=ALU.mult, inc=s_ve,
                            waits=[(s_ve, tbc), (s_tm, last_st)])
                    qk_done[p] = last_st
                    pv_done[p] = last_st
                    qb_free = max(qb_free or 0, (dn_free[1] or (None, 0))[1])

                proj_qkv_norms(0)
                for p in range(NP):
                    if p + 1 < NP:
                        proj_qkv_norms(p + 1)
                    attn(p)
                ao_done = s_ve.v
                if DEBUG:
                    dsem = self.sem("lc0")
                    E_.dma("gpsimd", self.dbg[:, 0:16], negb[:],
                           dsem, waits=[(s_ve, ao_done)])
                    E_.dma("gpsimd", self.dbg[0:2, 16:32], nrm[0:2, 0:16],
                           dsem)
                    E_.dma("gpsimd", self.dbg[0:2, 32:40], bnd[0:2, :], dsem)
                    E_.dma("gpsimd", self.dbg[:, 64:2112], Qp[0][:], dsem)
                    E_.dma("gpsimd", self.dbg[:, 2112:4160], Kp[0][:], dsem)
                    E_.dma("gpsimd", self.dbg[:, 4160:6208], Vp[0][:, 0:2048], dsem)
                    E_.dma("gpsimd", self.dbg[:, 6208:22592],
                           aoT[:], dsem)
                    E_.dma("gpsimd", self.dbg[:, 22592:25664],
                           pjT[0][:], dsem)

                t_all_attn = s_tm.v
                tr_tok = [None] * DC
                opv = opT.ap().rearrange("p (dc d) -> p dc d", dc=DC)
                for c in range(DC):
                    tpo = E_.dma(
                        "gpsimd", pjn[:, (c % 2) * D:(c % 2 + 1) * D],
                        self.opw[c * 128:(c + 1) * 128, :], s_pj[c % 2],
                        waits=[(s_tm, t_all_attn if c < 2
                                else tr_tok[c - 2])])
                    for dcg in range(2):
                        for dcl in range(4):
                            dc = dcg * 4 + dcl
                            E_.mm(qkb[:, dcl * 128:(dcl + 1) * 128],
                                  pjn[:, (c % 2) * D + dc * 128:
                                      (c % 2) * D + (dc + 1) * 128],
                                  self.ident[:], start=True, stop=True,
                                  inc=s_tm if dcl == 3 else None,
                                  waits=[(s_pj[c % 2], tpo),
                                         (s_ve, qb_free)])
                        tr_tok[c] = s_tm.v
                        qb_free = E_.vop(
                            "tensor_copy",
                            opv[:, dcg * 4:(dcg + 1) * 4,
                                c * 128:(c + 1) * 128],
                            qkb.ap().rearrange("p (a b) -> p a b", a=4),
                            inc=s_ve, waits=[(s_tm, tr_tok[c])])
                opT_rdy = qb_free

                lr_tok = [None, None]
                x1_tok = [None, None]
                ev_prev = [None, None]
                x2_tok = [None, None]
                po_tok = None
                for tt in range(NTT):
                    pp = tt % 2
                    x1_tok[pp] = E_.dma(
                        "sync", x1t[pp][:],
                        src[tt * 128:(tt + 1) * 128, :], s_x[pp],
                        serial=True,
                        waits=[(s_ve, ev_prev[pp]), (s_tm, ttt_tm),
                               (s_ve, ttt_ve), (s_se, ttt_se)])
                    lr_tok[pp] = E_.dma(
                        "gpsimd", lct[pp][:],
                        self.lacts[tt * 128:(tt + 1) * 128, :], s_lr[pp],
                        waits=[(s_lc[0], s_lc[0].v), (s_lc[1], s_lc[1].v),
                               (s_ve, ev_prev[pp]), (s_tm, ttt_tm),
                               (s_ve, ttt_ve), (s_se, ttt_se)])
                    for dh in range(2):
                        pso = stAB[0] if dh == 0 else stAB[1]
                        pso = pso.ap()[:, 0:512] if False else pso[:, 0:512]
                        for dc in range(DC):
                            E_.mm(pso[:],
                                  aoT[:, dc * T + tt * 128:
                                      dc * T + (tt + 1) * 128],
                                  opT[:, dc * D + dh * 512:
                                      dc * D + (dh + 1) * 512],
                                  start=(dc == 0), stop=(dc == DC - 1),
                                  inc=s_tm if dc == DC - 1 else None,
                                  waits=[(s_ve, opT_rdy),
                                         (s_ve, ao_done),
                                         (s_ve, ev_prev[pp]),
                                         (s_se, s_se.v)])
                        tmm = s_tm.v
                        t1 = E_.vop("tensor_tensor",
                                    x2t[pp][:, dh * 512:(dh + 1) * 512],
                                    pso[:], obc[:, dh * 512:(dh + 1) * 512],
                                    op=ALU.add, inc=s_ve,
                                    waits=[(s_tm, tmm), (s_ve, t_bcast),
                                           (s_xe[pp], x2_tok[pp])])
                        t2 = E_.vop("tensor_tensor",
                                    x2t[pp][:, dh * 512:(dh + 1) * 512],
                                    x2t[pp][:, dh * 512:(dh + 1) * 512],
                                    lct[pp][:, dh * 512:(dh + 1) * 512],
                                    op=ALU.add, inc=s_ve,
                                    waits=[(s_ve, t1),
                                           (s_lr[pp], lr_tok[pp])])
                        ev_prev[pp] = E_.vop(
                            "tensor_tensor",
                            x2t[pp][:, dh * 512:(dh + 1) * 512],
                            x2t[pp][:, dh * 512:(dh + 1) * 512],
                            x1t[pp][:, dh * 512:(dh + 1) * 512],
                            op=ALU.add, inc=s_ve,
                            waits=[(s_ve, t2), (s_x[pp], x1_tok[pp])])
                    x2_tok[pp] = E_.dma(
                        "gpsimd", dst[tt * 128:(tt + 1) * 128, :],
                        x2t[pp][:], s_xe[pp], waits=[(s_ve, ev_prev[pp])])

                E_.wait_on("gpsimd", [(s_xe[0], s_xe[0].v),
                                      (s_xe[1], s_xe[1].v),
                                      (s_lr[0], s_lr[0].v),
                                      (s_lr[1], s_lr[1].v),
                                      (s_lc[0], s_lc[0].v),
                                      (s_lc[1], s_lc[1].v),
                                      (s_pj[0], s_pj[0].v),
                                      (s_pj[1], s_pj[1].v),
                                      (s_tw, s_tw.v)])
                E_.wait_on("sync", [(s_x[0], s_x[0].v),
                                    (s_x[1], s_x[1].v)])

            E_.run("attn_blk", gen)
            st3.close()
            pst3.close()


def _mk_selb():
    s = np.zeros((2, 256), dtype=ml_dtypes.bfloat16)
    s[0, 0:128] = 1
    s[1, 128:256] = 1
    return s


def _mk_self():
    s = np.zeros((2, 128), dtype=np.float32)
    s[0, 0:64] = 1
    s[1, 64:128] = 1
    return s


_BUILT = None


def kernel(L0_attn_x, L0_attn_ln_w, L0_attn_ln_b, L0_attn_in_proj_w,
           L0_attn_in_proj_b, L0_attn_out_proj_w, L0_attn_out_proj_b,
           L0_attn_w0, L0_attn_w1, L0_attn_w2,
           L0_ffn_Wg, L0_ffn_W1, L0_ffn_W2, L0_ffn_W3):
    x = np.ascontiguousarray(np.asarray(L0_attn_x, dtype=np.float32))
    B = x.shape[0]
    nc = Builder().build()
    common = {
        "ffn_wg": np.ascontiguousarray(L0_ffn_Wg, np.float32),
        "ffn_w1": np.ascontiguousarray(L0_ffn_W1, np.float32),
        "ffn_w2": np.ascontiguousarray(L0_ffn_W2, np.float32),
        "ffn_w3": np.ascontiguousarray(L0_ffn_W3, np.float32),
        "ln_w": np.ascontiguousarray(L0_attn_ln_w, np.float32),
        "ln_b": np.ascontiguousarray(L0_attn_ln_b, np.float32),
        "in_proj_w": np.ascontiguousarray(L0_attn_in_proj_w, np.float32),
        "in_proj_b": np.ascontiguousarray(L0_attn_in_proj_b, np.float32),
        "out_proj_w": np.ascontiguousarray(L0_attn_out_proj_w, np.float32),
        "out_proj_b": np.ascontiguousarray(L0_attn_out_proj_b, np.float32),
        "ttt_w0": np.ascontiguousarray(L0_attn_w0, np.float32),
        "ttt_w1": np.ascontiguousarray(L0_attn_w1, np.float32),
        "ttt_w2": np.ascontiguousarray(L0_attn_w2, np.float32),
        "identD": np.eye(128, dtype=ml_dtypes.bfloat16),
        "selbD": _mk_selb(),
        "selfD": _mk_self(),
    }
    in_maps = [dict(common, x=np.ascontiguousarray(x[b])) for b in range(B)]
    res = run_bass_kernel_spmd(nc, in_maps, core_ids=list(range(B)))
    out = np.stack([np.asarray(r["y"], np.float32) for r in res.results], 0)
    return out
